# version 13
# baseline (speedup 1.0000x reference)
import contextlib
import numpy as np
import ml_dtypes
import concourse.bass as bass
import concourse.mybir as mybir
from concourse.bass_utils import run_bass_kernel_spmd

F32 = mybir.dt.float32
F16 = mybir.dt.float16
BF = mybir.dt.bfloat16
I32 = mybir.dt.int32
AF = mybir.ActivationFunctionType
ALU = mybir.AluOpType
AX = mybir.AxisListType

D = 1024
DH = 4096
T = 4096
WM = 192
XR = 4352
SI = 64
SJ = 68
SEG = 64
PW = SI * 8 * SJ // 8
GW = T + 2
MT = 256
NMT = T // MT
EPS = 1e-5


class _Op:
    __slots__ = ("eng", "fn", "waits", "incs", "sync_val", "dma_tok", "ldw")

    def __init__(self, eng, fn, waits):
        self.eng, self.fn, self.waits = eng, fn, waits
        self.incs = []
        self.sync_val = None
        self.dma_tok = None
        self.ldw = None


class Sched:
    ENGS = ("sync", "tensor", "vector", "scalar", "gpsimd")

    def __init__(self, nc, es):
        self.nc, self.es = nc, es
        self.ops = []
        self.last = {e: None for e in self.ENGS}
        self.sync_count = {e: 0 for e in self.ENGS}
        self.dma_count = {}
        self.sems = {}

    def _sem(self, name):
        if name not in self.sems:
            self.sems[name] = self.es.enter_context(self.nc.semaphore(name))
        return self.sems[name]

    def op(self, eng, fn, deps=(), dma=None, ldw=None):
        waits = [t for t in deps if t is not None]
        o = _Op(eng, fn, waits)
        o.ldw = ldw
        self.ops.append(o)
        self.last[eng] = len(self.ops) - 1
        if dma is not None:
            self.dma_count[dma] = self.dma_count.get(dma, 0) + 16
            o.incs.append((dma, 16))
            o.dma_tok = (dma, self.dma_count[dma])
        return len(self.ops) - 1

    def dop(self, idx):
        o = self.ops[idx]
        if o.dma_tok is not None:
            return o.dma_tok
        o.sync_val = -1
        return ("OPREF", idx)

    def d(self, eng):
        li = self.last[eng]
        return None if li is None else self.dop(li)

    def ddma(self, name):
        c = self.dma_count.get(name, 0)
        return None if c == 0 else (name, c)

    def barrier_deps(self):
        return [self.d(e) for e in self.ENGS if self.last[e] is not None] + \
               [self.ddma(n) for n in self.dma_count]

    def emit(self, blk):
        for n in list(self.dma_count):
            self._sem(n)
        for e in self.ENGS:
            self._sem("E_" + e)
        counts = {e: 0 for e in self.ENGS}
        for i, o in enumerate(self.ops):
            if o.sync_val is not None:
                counts[o.eng] += 1
                o.sync_val = counts[o.eng]
                o.incs.append(("E_" + o.eng, 1))
        for o in self.ops:
            rw = []
            for sn, val in o.waits:
                if sn == "OPREF":
                    t = self.ops[val]
                    rw.append(("E_" + t.eng, t.sync_val))
                else:
                    rw.append((sn, val))
            o.waits = rw
        eng_ops = {e: [o for o in self.ops if o.eng == e] for e in self.ENGS}

        def runner(ename):
            def run(eng):
                waited = {}
                for o in eng_ops[ename]:
                    for sn, val in o.waits:
                        if waited.get(sn, -1) >= val:
                            continue
                        eng.wait_ge(self.sems[sn], val)
                        waited[sn] = val
                    if o.fn is None:
                        continue
                    if o.ldw is not None:
                        ap, istr = o.ldw
                        if istr:
                            eng.ldweights(ap, is_transpose=True)
                        else:
                            eng.ldweights(ap)
                    inst = o.fn(eng)
                    for sn, n in o.incs:
                        inst.then_inc(self.sems[sn], n)
            return run

        blk.sync(runner("sync"))
        blk.tensor(runner("tensor"))
        blk.vector(runner("vector"))
        blk.scalar(runner("scalar"))
        blk.gpsimd(runner("gpsimd"))


def build_nc(debug=False):
    nc = bass.Bass("TRN2", target_bir_lowering=False)
    es = contextlib.ExitStack()
    nc._keepalive = es

    def din(name, shape, dt):
        return nc.dram_tensor(name, shape, dt, kind="ExternalInput").ap()

    xq = din("xq", [XR, D], F32)
    wst = din("wst", [D, D], F32)
    wgt = din("wgt", [D, D], F32)
    bst_in = din("bst_in", [D], F32)
    bgt_in = din("bgt_in", [D], F32)
    w1 = din("w1", [DH, D], F32)
    cw = din("cw", [DH, 3], F32)
    cbv = din("cbv", [DH], F32)
    w2 = din("w2", [D, DH], F32)
    cvec = din("cvec", [D], F32)
    wmod = din("wmod", [D, 6 * D], F32)
    identb_in = din("identb_in", [128, 128], BF)
    ident32_in = din("ident32_in", [128, 128], F32)
    segmask_in = din("segmask_in", [128, 8 * SEG], F16)
    edgemask_in = din("edgemask_in", [2, 1], F32)

    out_d = nc.dram_tensor("out", [T, D], F32, kind="ExternalOutput").ap()
    mods = nc.dram_tensor("mods", [6 * D], F32, kind="Internal").ap()
    kw1d = nc.dram_tensor("kw1d", [128, 8 * DH], BF, kind="Internal").ap()
    kw2d = nc.dram_tensor("kw2d", [128, 32 * D], BF, kind="Internal").ap()
    x2s = nc.dram_tensor("x2s", [T, D], F32,
                         kind="ExternalOutput" if debug else "Internal").ap()
    kxs = nc.dram_tensor("kxs", [8, 128, GW], BF,
                         kind="ExternalOutput" if debug else "Internal").ap()
    kxsr = kxs.rearrange("j p c -> p j c")
    if debug:
        dbg_px = nc.dram_tensor("dbg_px", [128, 8 * PW], F16,
                                kind="ExternalOutput").ap()
        dbg_g = nc.dram_tensor("dbg_g", [128, 8 * GW], F16,
                               kind="ExternalOutput").ap()
        dbg_y = nc.dram_tensor("dbg_y", [128, 32 * 258], BF,
                               kind="ExternalOutput").ap()
        dbg_xst = nc.dram_tensor("dbg_xst", [128, 2 * 4 * D], BF,
                                 kind="ExternalOutput").ap()
        dbg_xnf = nc.dram_tensor("dbg_xnf", [128, 2 * 8 * 512], BF,
                                 kind="ExternalOutput").ap()

    sc = Sched(nc, es)
    OP, DEP, DOP, DMA = sc.op, sc.d, sc.dop, sc.ddma

    def sb(name, shape, dt, stack, side="left"):
        return stack.enter_context(nc.sbuf_tensor(name, shape, dt, side=side))

    identb = sb("identb", [128, 128], BF, es)
    ident32 = sb("ident32", [128, 128], F32, es)
    ones_r = sb("ones_r", [1, 128], F32, es)
    one_c = sb("one_c", [128, 1], F32, es)
    eps_c = sb("eps_c", [128, 1], F32, es)
    cvec_c = sb("cvec_c", [128, 8], F32, es)
    silc_c = sb("silc_c", [128, 8], F32, es)
    capca_c = sb("capca_c", [128, 8], F32, es)
    uvec_c = sb("uvec_c", [128, 8], BF, es)
    uvec_f = sb("uvec_f", [128, 8], F32, es)
    ca_col = sb("ca_col", [128, 8], F32, es)
    sa_col = sb("sa_col", [128, 8], F32, es)
    bst_c = sb("bst_c", [128, 8], F32, es)
    bgt_c = sb("bgt_c", [128, 8], F32, es)
    cw_c = sb("cw_c", [128, 96], F32, es)
    cb_c = sb("cb_c", [128, 32], F32, es)
    swc1 = sb("swc1", [128, 1], F32, es)
    sq1_c = sb("sq1_c", [128, 1], F32, es)
    sq2_c = sb("sq2_c", [128, 1], F32, es)
    swc2 = sb("swc2", [128, 1], F32, es)
    edgemask = sb("edgemask", [2, 1], F32, es)
    gb_t = sb("gb_t", [128, D], F32, es, side="right")
    sw_r = sb("sw_r", [1, 1], F32, es)
    acc_c = sb("acc_c", [128, 1], F32, es)
    sedge = sb("sedge", [128, 8], F32, es)
    red_c = sb("red_c", [128, 1], F32, es)

    pm0 = es.enter_context(nc.psum_tensor("pm0", [128, 512], F32))
    pm1 = es.enter_context(nc.psum_tensor("pm1", [128, 512], F32))
    pt0 = es.enter_context(nc.psum_tensor("pt0", [128, 512], F32))
    pt1 = es.enter_context(nc.psum_tensor("pt1", [128, 512], F32))
    pb0 = es.enter_context(nc.psum_tensor("pb0", [128, 512], F32))
    pb1 = es.enter_context(nc.psum_tensor("pb1", [128, 512], F32))
    ptb0 = es.enter_context(nc.psum_tensor("ptb0", [128, 1024], BF))
    ptb1 = es.enter_context(nc.psum_tensor("ptb1", [128, 1024], BF))
    bank_guard = {id(p): [] for p in (pm0, pm1, pt0, pt1, pb0, pb1, ptb0, ptb1)}

    def t_write(p, fn, deps=(), ldw=None):
        return OP("tensor", fn, list(deps) + bank_guard[id(p)], ldw=ldw)

    def t_acc(fn, ldw=None):
        return OP("tensor", fn, ldw=ldw)

    def consume(p, eng, fn, deps=()):
        i = OP(eng, fn, deps)
        bank_guard[id(p)] = [DOP(i)]
        return i

    s1 = contextlib.ExitStack()
    px = sb("px", [128, SI * 8 * SJ], F16, s1)
    g = sb("g", [128, 8 * GW], F16, s1)
    pxi = px.rearrange("p (i j s) -> p i j s", i=SI, j=8)
    gr = g.rearrange("p (j c) -> p j c", j=8)

    OP("sync", lambda e: e.dma_start(out=identb[:, :], in_=identb_in), dma="DL")
    OP("sync", lambda e: e.dma_start(out=ident32[:, :], in_=ident32_in), dma="DL")
    OP("sync", lambda e: e.dma_start(out=edgemask[:, :], in_=edgemask_in), dma="DL")
    OP("sync", lambda e: e.dma_start(
        out=cvec_c[:, :], in_=cvec.rearrange("(f p) -> p f", p=128), allow_slow_non_contiguous=True), dma="DL")
    OP("sync", lambda e: e.dma_start(
        out=cw_c.rearrange("p (f k) -> p f k", k=3), in_=cw.rearrange("(f p) k -> p f k", p=128), allow_slow_non_contiguous=True), dma="DL")
    OP("sync", lambda e: e.dma_start(
        out=cb_c[:, :], in_=cbv.rearrange("(f p) -> p f", p=128), allow_slow_non_contiguous=True), dma="DL")
    ld0 = DMA("DL")
    OP("vector", lambda e: e.memset(ones_r[:, :], 1.0))
    OP("vector", lambda e: e.memset(one_c[:, :], 1.0))
    OP("vector", lambda e: e.memset(eps_c[:, :], EPS))
    OP("scalar", lambda e: e.activation(silc_c[:, :], cvec_c[:, :], AF.Silu),
       deps=[ld0])
    sil_done = DEP("scalar")

    smod = contextlib.ExitStack()
    wmst = [sb("wmst0", [128, 3 * D // 2], F32, smod, side="right"),
            sb("wmst1", [128, 3 * D // 2], F32, smod, side="right")]
    modc = sb("modc", [128, 48], F32, smod, side="right")
    modr2 = sb("modr2", [48, 128], F32, smod, side="right")
    gbrow = sb("gbrow", [1, D], F32, smod, side="right")

    s1w = contextlib.ExitStack()
    wts = sb("wts", [128, 8 * D], BF, s1w)
    wtg = sb("wtg", [128, 8 * D], BF, s1w)
    sw = contextlib.ExitStack()
    wrow = [sb("wrow0", [128, D], F32, sw, side="right"),
            sb("wrow1", [128, D], F32, sw, side="right")]

    mm_guard = [None, None]
    tp_guard = [None, None]
    first_mod = [True]

    def mod_piece(gi, j, qh):
        par = gi % 2
        OP("sync", lambda e, j=j, qh=qh, par=par: e.dma_start(
            out=wmst[par][:, :],
            in_=wmod[j * 128:(j + 1) * 128,
                     qh * 3 * D // 2:(qh + 1) * 3 * D // 2]),
            dma="DL", deps=[mm_guard[par]])
        wdep = DMA("DL")
        last = None
        for cc in range(12):
            ch = qh * 12 + cc
            fn = (lambda e, cc=cc, ch=ch, j=j, par=par: e.matmul(
                pb0[:, (ch * 8 + j):(ch * 8 + j) + 1],
                wmst[par][:, cc * 128:(cc + 1) * 128],
                silc_c[:, j:j + 1], start=True, stop=True))
            if first_mod[0]:
                last = t_write(pb0, fn, [wdep, sil_done])
                first_mod[0] = False
            elif cc == 0:
                last = OP("tensor", fn, [wdep, sil_done])
            else:
                last = t_acc(fn)
        mm_guard[par] = DOP(last)

    def wt_piece(gi2, wsrc, dst, i):
        par = gi2 % 2
        stg = wrow[par]
        OP("sync", lambda e, i=i, stg=stg, wsrc=wsrc: e.dma_start(
            out=stg[:, :], in_=wsrc[i * 128:(i + 1) * 128, :]), dma="DL",
            deps=[tp_guard[par]])
        wdep = DMA("DL")
        last = None
        for j in range(8):
            p = pt0 if j % 2 == 0 else pt1
            last = t_write(p, lambda e, j=j, p=p, stg=stg: e.transpose(
                p[:, 0:128], stg[:, j * 128:(j + 1) * 128], ident32[:, :]),
                deps=[wdep, ld0])
            consume(p, "vector",
                    lambda e, i=i, j=j, p=p, dst=dst: e.tensor_copy(
                        dst[:, j * D + i * 128:j * D + (i + 1) * 128],
                        p[:, 0:128]),
                    deps=[DEP("tensor")])
        tp_guard[par] = DOP(last)

    wt_jobs = [(wsrc, dst, i) for wsrc, dst in ((wst, wts), (wgt, wtg))
               for i in range(8)]
    gi = 0
    gi2 = 0
    for j in range(8):
        for qh in range(4):
            mod_piece(gi, j, qh)
            gi += 1
            if gi % 2 == 0 and gi2 < 16:
                wt_piece(gi2, *wt_jobs[gi2])
                gi2 += 1
    while gi2 < 16:
        wt_piece(gi2, *wt_jobs[gi2])
        gi2 += 1
    sw.close()
    wt_done = DEP("vector")

    consume(pb0, "vector", lambda e: e.reduce_sum(
        modc[:, :], pb0[:, 0:384].rearrange("p (c j) -> p c j", j=8),
        axis=AX.X), deps=[DEP("tensor")])
    OP("vector", lambda e: e.drain())
    t_write(pt0, lambda e: e.transpose(pt0[0:48, 0:128], modc[:, 0:48],
                                       ident32[:, :]), deps=[DEP("vector")])
    consume(pt0, "vector", lambda e: e.tensor_copy(modr2[:, :],
                                                   pt0[0:48, 0:128]),
            deps=[DEP("tensor")])
    OP("vector", lambda e: e.drain())
    OP("sync", lambda e: e.dma_start(out=mods.rearrange("(a b) -> a b", a=48),
                                     in_=modr2[:, :]), dma="DS",
       deps=[DEP("vector")])
    mods_out = DMA("DS")
    OP("sync", lambda e: e.dma_start(
        out=ca_col[:, :], in_=mods[D:2 * D].rearrange("(f p) -> p f", p=128),
        allow_slow_non_contiguous=True), dma="DL", deps=[mods_out])
    OP("sync", lambda e: e.dma_start(
        out=sa_col[:, :], in_=mods[0:D].rearrange("(f p) -> p f", p=128),
        allow_slow_non_contiguous=True), dma="DL")
    cols_in = DMA("DL")
    OP("vector", lambda e: e.tensor_scalar(capca_c[:, :], ca_col[:, :], 1.0, None,
                                           ALU.add), deps=[cols_in])
    OP("vector", lambda e: e.drain())
    OP("vector", lambda e: e.tensor_copy(uvec_c[:, :], sa_col[:, :]))
    OP("vector", lambda e: e.drain())
    caps_done = DEP("vector")

    def gate_tile_transform(tile, scr):
        OP("vector", lambda e: e.tensor_scalar(scr, tile, -1.0, None, ALU.mult))
        OP("vector", lambda e: e.tensor_tensor(scr, tile, scr, ALU.max))
        OP("vector", lambda e: e.tensor_scalar(scr, scr, 0.25, None, ALU.max))
        OP("vector", lambda e: e.tensor_single_scalar(tile, tile, 0.0, ALU.is_ge))
        OP("vector", lambda e: e.tensor_scalar(tile, tile, 2.0, -1.0, ALU.mult,
                                               ALU.add))
        OP("vector", lambda e: e.tensor_tensor(tile, tile, scr, ALU.mult))
        OP("vector", lambda e: e.drain())

    OP("sync", lambda e: e.dma_start(
        out=gbrow[:, :], in_=mods[5 * D:6 * D].rearrange("(a b) -> a b", a=1)),
        dma="DL", deps=[mods_out])
    grow_in = DMA("DL")
    for h in range(2):
        p = pb0 if h == 0 else pb1
        t_write(p, lambda e, h=h, p=p: e.matmul(
            p[:, :], ones_r[:, :], gbrow[:, h * 512:(h + 1) * 512],
            start=True, stop=True), deps=[grow_in])
        consume(p, "vector", lambda e, h=h, p=p: e.tensor_copy(
            gb_t[:, h * 512:(h + 1) * 512], p[:, :]), deps=[DEP("tensor")])
    gate_tile_transform(gb_t[:, :], wmst[0][:, 0:D])

    smod.close()

    OP("sync", lambda e: e.dma_start(
        out=bst_c[:, :], in_=bst_in.rearrange("(f p) -> p f", p=128), allow_slow_non_contiguous=True), dma="DL")
    OP("sync", lambda e: e.dma_start(
        out=bgt_c[:, :], in_=bgt_in.rearrange("(f p) -> p f", p=128), allow_slow_non_contiguous=True), dma="DL")
    bld = DMA("DL")

    for wt_t, dstb in ((wts, bst_c), (wtg, bgt_c)):
        for k in range(8):
            p = pb0 if k % 2 == 0 else pb1
            for j in range(8):
                lw = (wt_t[:, j * D + k * 128:j * D + (k + 1) * 128], False)
                fn = (lambda e, j=j, k=k, p=p, wt_t=wt_t: e.matmul(
                    p[:, 0:1], wt_t[:, j * D + k * 128:j * D + (k + 1) * 128],
                    uvec_c[:, j:j + 1], start=(j == 0), stop=(j == 7)))
                if j == 0:
                    t_write(p, fn, [wt_done, caps_done], ldw=lw)
                else:
                    t_acc(fn, ldw=lw)
            consume(p, "vector", lambda e, k=k, p=p, dstb=dstb: e.tensor_tensor(
                dstb[:, k:k + 1], dstb[:, k:k + 1], p[:, 0:1], ALU.add),
                deps=[DEP("tensor"), bld])

    def ln_stats_norm(src_ap, wdep, part, dst_ap, bufs, dst_scalar_dep=None):
        negm_, ssq_, var_, ist_, nms_, t1_, sqs_ = bufs
        dd = [wdep] if wdep is not None else []
        rr = OP("vector", lambda e: e.reduce_sum(negm_, src_ap, axis=AX.X,
                                                 negate=True),
                deps=dd + [DEP("gpsimd")])
        OP("scalar", lambda e: e.activation(sqs_, src_ap, AF.Square,
                                            accum_out=ssq_),
           deps=dd + ([dst_scalar_dep] if dst_scalar_dep else []))
        sdep = DEP("scalar")
        OP("gpsimd", lambda e: e.tensor_scalar(negm_, negm_, 1.0 / D, None,
                                               ALU.mult), deps=[DOP(rr)])
        OP("gpsimd", lambda e: e.tensor_tensor(var_, negm_, negm_, ALU.mult))
        OP("gpsimd", lambda e: e.tensor_scalar(t1_, ssq_, 1.0 / D, None,
                                               ALU.mult), deps=[sdep])
        OP("gpsimd", lambda e: e.tensor_tensor(var_, t1_, var_, ALU.subtract))
        OP("scalar", lambda e, part=part: e.activation(
            t1_, var_, AF.Sqrt, bias=eps_c[0:part, 0:1]), deps=[DEP("gpsimd")])
        OP("vector", lambda e: e.reciprocal(ist_, t1_), deps=[DEP("scalar")])
        OP("gpsimd", lambda e: e.tensor_tensor(nms_, negm_, ist_, ALU.mult),
           deps=[DEP("vector")])
        gdep = DEP("gpsimd")
        OP("scalar", lambda e: e.activation(dst_ap, src_ap, AF.Identity,
                                            bias=nms_, scale=ist_),
           deps=[gdep])
        return DEP("scalar")

    sa = contextlib.ExitStack()
    xstF4 = sb("xstF4", [128, 4 * D], F32, sa)
    xF = xstF4.rearrange("p (s c) -> p s c", c=D)
    xnb = [sb("xnb0", [128, D], BF, sa), sb("xnb1", [128, D], BF, sa)]
    xnf2 = [sb("xnfa", [128, 8 * 512], BF, sa), sb("xnfb", [128, 8 * 512], BF, sa)]
    stats = [[sb(f"st{n}{p_}", [128, 4], F32, sa) for n in range(5)]
             for p_ in range(2)]

    tiles_a = [(ti * 512, 512) for ti in range(8)] + [(4096, 256)]
    xnf_reader = [None, None]
    xnb_reader = [None, None]
    xf_readers = [None] * 4
    ln_prev = [None, None]
    for ti, (c0, width) in enumerate(tiles_a):
        par = ti % 2
        nsub = width // 128
        xnf = xnf2[par]
        xnfr = xnf.rearrange("p (j c) -> p j c", j=8)
        negm4, ssq4, var4, ist4, nm4 = stats[par]
        wdeps = []
        for u in range(nsub):
            OP("sync", lambda e, c0=c0, u=u: e.dma_start(
                out=xF[:, u, :], in_=xq[c0 + u * 128:c0 + (u + 1) * 128, :]),
                dma="DL", deps=[xf_readers[u]])
            wdeps.append(DMA("DL"))
        for u in range(nsub):
            wdep = wdeps[u]
            rd_i = OP("vector", lambda e, u=u, negm4=negm4: e.reduce_sum(
                negm4[:, u:u + 1], xF[:, u, :], axis=AX.X, negate=True),
                deps=[wdep] + ([ln_prev[par]] if u == 0 else []))
            OP("scalar", lambda e, u=u, negm4=negm4, ssq4=ssq4: e.activation(
                xnb[(u + 1) % 2][:, :], xF[:, u, :], AF.Square,
                accum_out=ssq4[:, u:u + 1]),
                deps=[wdep, xnb_reader[(u + 1) % 2]]
                     + ([ln_prev[par]] if u == 0 else []))
        vred = DEP("vector")
        sdep = DEP("scalar")
        OP("gpsimd", lambda e, negm4=negm4, nsub=nsub: e.tensor_scalar(
            negm4[:, 0:nsub], negm4[:, 0:nsub], 1.0 / D, None, ALU.mult),
            deps=[vred])
        OP("gpsimd", lambda e, negm4=negm4, var4=var4, nsub=nsub: e.tensor_tensor(
            var4[:, 0:nsub], negm4[:, 0:nsub], negm4[:, 0:nsub], ALU.mult))
        OP("gpsimd", lambda e, ssq4=ssq4, nm4=nm4, nsub=nsub: e.tensor_scalar(
            nm4[:, 0:nsub], ssq4[:, 0:nsub], 1.0 / D, None, ALU.mult),
            deps=[sdep])
        OP("gpsimd", lambda e, nm4=nm4, var4=var4, nsub=nsub: e.tensor_tensor(
            var4[:, 0:nsub], nm4[:, 0:nsub], var4[:, 0:nsub], ALU.subtract))
        OP("scalar", lambda e, var4=var4, nm4=nm4, nsub=nsub: e.activation(
            nm4[:, 0:nsub], var4[:, 0:nsub], AF.Sqrt, bias=eps_c[:, 0:1]),
            deps=[DEP("gpsimd")])
        OP("vector", lambda e, nm4=nm4, ist4=ist4, nsub=nsub: e.reciprocal(
            ist4[:, 0:nsub], nm4[:, 0:nsub]), deps=[DEP("scalar")])
        OP("gpsimd", lambda e, negm4=negm4, ist4=ist4, nm4=nm4, nsub=nsub:
            e.tensor_tensor(nm4[:, 0:nsub], negm4[:, 0:nsub], ist4[:, 0:nsub],
                            ALU.mult), deps=[DEP("vector")])
        vdep = DEP("gpsimd")
        last_f = None
        for u in range(nsub):
            xb = xnb[u % 2]
            nrm = OP("scalar", lambda e, u=u, xb=xb, ist4=ist4, nm4=nm4:
                     e.activation(xb[:, :], xF[:, u, :], AF.Identity,
                                  bias=nm4[:, u:u + 1], scale=ist4[:, u:u + 1]),
                     deps=[vdep, xnb_reader[u % 2]])
            last_f = nrm
            xf_readers[u] = DOP(nrm)
            last_tp = None
            for ph in range(2):
                p = ptb0 if ph == 0 else ptb1
                for jj in range(4):
                    j = ph * 4 + jj
                    lw = (xb[:, j * 128:(j + 1) * 128], True)
                    fn = (lambda e, j=j, jj=jj, p=p, xb=xb: e.transpose(
                        p[:, jj * 128:(jj + 1) * 128],
                        xb[:, j * 128:(j + 1) * 128], identb[:, :]))
                    if jj == 0:
                        last_tp = t_write(p, fn, [DOP(nrm)], ldw=lw)
                    else:
                        last_tp = t_acc(fn, ldw=lw)
                for jj in range(4):
                    j = ph * 4 + jj
                    fn = (lambda e, j=j, jj=jj, u=u, p=p, xnfr=xnfr:
                          e.tensor_scalar(
                              xnfr[:, j, u * 128:(u + 1) * 128],
                              p[:, jj * 128:(jj + 1) * 128],
                              capca_c[:, j:j + 1], None, ALU.mult))
                    if jj < 3:
                        OP("vector", fn,
                           deps=([DEP("tensor"), xnf_reader[par], caps_done]
                                 if jj == 0 else []))
                    else:
                        consume(p, "vector", fn)
            xnb_reader[u % 2] = DOP(last_tp)
        ln_prev[par] = DOP(last_f)
        xnf_done = DEP("vector")
        last_mm = None
        for wt_t, b_c, is_g in ((wts, bst_c, False), (wtg, bgt_c, True)):
            for k in range(8):
                p = pm0 if k % 2 == 0 else pm1
                for j in range(8):
                    lw = (wt_t[:, j * D + k * 128:j * D + (k + 1) * 128], False)
                    fn = (lambda e, j=j, k=k, p=p, wt_t=wt_t, width=width,
                          xnf=xnf: e.matmul(
                              p[:, 0:width],
                              wt_t[:, j * D + k * 128:j * D + (k + 1) * 128],
                              xnf[:, j * 512:j * 512 + width],
                              start=(j == 0), stop=(j == 7)))
                    if j == 0:
                        last_mm = t_write(p, fn, [xnf_done], ldw=lw)
                    else:
                        last_mm = t_acc(fn, ldw=lw)
                tdep = DEP("tensor")
                if not is_g:
                    st0, nst = c0 // SI, width // SI
                    consume(p, "scalar", lambda e, k=k, p=p, st0=st0, nst=nst,
                            width=width: e.activation(
                                pxi[:, :, k, st0:st0 + nst].rearrange(
                                    "p i s -> p s i"),
                                p[:, 0:width],
                                AF.Identity, bias=bst_c[:, k:k + 1]),
                            deps=[tdep])
                else:
                    lo, hi = max(c0, WM - 1), min(c0 + width, WM - 1 + GW)
                    if lo < hi:
                        consume(p, "scalar",
                                lambda e, k=k, p=p, c0=c0, lo=lo, hi=hi:
                                e.activation(gr[:, k, lo - (WM - 1):hi - (WM - 1)],
                                             p[:, lo - c0:hi - c0], AF.Sigmoid,
                                             bias=bgt_c[:, k:k + 1]),
                                deps=[tdep])
        xnf_reader[par] = DOP(last_mm)
    sa.close()
    s1w.close()
    a_done = DEP("scalar")

    smean = contextlib.ExitStack()
    mst = [sb(f"mstv{v}", [128, D], F32, smean, side="right")
           for v in range(2)]

    def meanabs(tiles, total, inv_dst, scale_dst):
        OP("gpsimd", lambda e: e.memset(acc_c[:, :], 0.0), deps=[a_done])
        readers = [None, None]
        for i, src_ in enumerate(tiles):
            stg = mst[i % 2]
            OP("sync", lambda e, src_=src_, stg=stg: e.dma_start(
                out=stg[:, :],
                in_=src_), dma="DL",
                deps=[readers[i % 2], a_done])
            wdep = DMA("DL")
            rr = OP("vector", lambda e, stg=stg: e.reduce_sum(
                red_c[:, :], stg[:, :], axis=AX.X, apply_absolute_value=True),
                deps=[wdep, DEP("gpsimd")])
            r = OP("gpsimd", lambda e: e.tensor_tensor(acc_c[:, :], acc_c[:, :],
                                                       red_c[:, :], ALU.add),
                   deps=[DOP(rr)])
            readers[i % 2] = DOP(r)
        gdep = DEP("gpsimd")
        t_write(pb0, lambda e: e.matmul(pb0[0:1, 0:1], acc_c[:, :], one_c[:, :],
                                        start=True, stop=True), deps=[gdep])
        consume(pb0, "vector", lambda e: e.tensor_scalar(
            sw_r[:, :], pb0[0:1, 0:1], 1.0 / total, None, ALU.mult),
            deps=[DEP("tensor")])
        OP("gpsimd", lambda e: e.tensor_scalar(sw_r[:, :], sw_r[:, :], EPS, None,
                                               ALU.max), deps=[DEP("vector")])
        gdep = DEP("gpsimd")
        t_write(pb1, lambda e: e.matmul(pb1[:, 0:1], ones_r[:, :], sw_r[:, :],
                                        start=True, stop=True), deps=[gdep])
        consume(pb1, "vector", lambda e: e.tensor_copy(scale_dst[:, :],
                                                       pb1[:, 0:1]),
                deps=[DEP("tensor")])
        OP("vector", lambda e: e.drain())
        OP("vector", lambda e: e.reciprocal(inv_dst[:, :], scale_dst[:, :]))
        OP("vector", lambda e: e.drain())

    w2v = w2.rearrange("a (n c) -> (a n) c", c=D)
    meanabs([w1[i * 128:(i + 1) * 128, :] for i in range(32)], DH * D,
            sq1_c, swc1)
    meanabs([w2v[i * 128:(i + 1) * 128, :] for i in range(32)], D * DH,
            sq2_c, swc2)
    smean.close()

    skw = contextlib.ExitStack()
    wq = [sb(f"wq{v}", [128, D], F32, skw) for v in range(4)]
    wiq = sb("wiq", [128, D], I32, skw)
    wqb = [sb(f"wqb{v}", [128, D], BF, skw) for v in range(4)]
    kstrip = [sb("kstrip0", [128, 512], BF, skw),
              sb("kstrip1", [128, 512], BF, skw)]

    def kw_gen():
        jobs = [([w1[i * 128:(i + 1) * 128, :] for i in range(32)],
                 sq1_c, kw1d, lambda j: j * DH)]
        for qq in range(4):
            jobs.append(([w2[i * 128:(i + 1) * 128, qq * D:(qq + 1) * D]
                          for i in range(8)],
                         sq2_c, kw2d, lambda j, qq=qq: (qq * 8 + j) * D))
        sdma = [None, None]
        si = 0
        for srcs, sq_col, dstd, chunk_base in jobs:
            for i4 in range(len(srcs) // 4):
                for v in range(4):
                    i = i4 * 4 + v
                    OP("sync", lambda e, src=srcs[i], stg=wq[v]: e.dma_start(
                        out=stg[:, :], in_=src), dma="DL",
                        deps=[DEP("tensor")])
                    wdep = DMA("DL")
                    OP("vector", lambda e, stg=wq[v], sq_col=sq_col:
                       e.tensor_scalar(wiq[:, :], stg[:, :], sq_col[:, 0:1],
                                       1.0, ALU.mult, ALU.min), deps=[wdep])
                    yield
                    OP("vector", lambda e, v=v: e.tensor_scalar(
                        wqb[v][:, :], wiq[:, :], -1.0, None, ALU.max))
                    yield
                qdep = DEP("vector")
                for j in range(8):
                    p = ptb0 if j % 2 == 0 else ptb1
                    for v in range(4):
                        lw = (wqb[v][:, j * 128:(j + 1) * 128], True)
                        fn = (lambda e, j=j, v=v, p=p: e.transpose(
                            p[:, v * 128:(v + 1) * 128],
                            wqb[v][:, j * 128:(j + 1) * 128], identb[:, :]))
                        if v == 0:
                            t_write(p, fn, [qdep], ldw=lw)
                        else:
                            t_acc(fn, ldw=lw)
                    st = kstrip[si % 2]
                    consume(p, "vector", lambda e, st=st, p=p: e.tensor_copy(
                        st[:, :], p[:, 0:512]),
                        deps=[DEP("tensor"), sdma[si % 2]])
                    cb_ = chunk_base(j)
                    OP("sync", lambda e, st=st, dstd=dstd, cb_=cb_, i4=i4:
                       e.dma_start(
                           out=dstd[:, cb_ + i4 * 512:cb_ + (i4 + 1) * 512],
                           in_=st[:, :]), dma="DG",
                       deps=[DEP("vector")])
                    sdma[si % 2] = DMA("DG")
                    si += 1
                    if si % 2 == 0:
                        yield

    kw_iter = kw_gen()

    ss = contextlib.ExitStack()
    ws = sb("ws", [128, 8 * SEG], F16, ss)
    tmp_h = [sb("tmph0", [128, 4 * SEG], F16, ss),
             sb("tmph1", [128, 4 * SEG], F16, ss)]
    etmp = sb("etmp", [128, 8], F32, ss)
    segmask = sb("segmask", [128, 8 * SEG], F16, ss)
    wsr = ws.rearrange("p (j s) -> p j s", j=8)
    OP("sync", lambda e: e.dma_start(out=segmask[:, :], in_=segmask_in),
       dma="DL", deps=[a_done])
    segm_ld = DMA("DL")

    halves = ((0, 4), (4, 8))
    sprev = [None, None]
    for w in range(WM):
        iw, dw = w % SI, w // SI
        for hi, (j0, j1) in enumerate(halves):
            tmp = tmp_h[hi]
            pslice = pxi[:, iw, j0:j1, dw:dw + SEG]
            wslice = wsr[:, j0:j1, :]
            if w == 0:
                v = OP("vector", lambda e, tmp=tmp, pslice=pslice: e.tensor_copy(
                    tmp[:, :], pslice), deps=[a_done])
            else:
                v = OP("vector", lambda e, tmp=tmp, pslice=pslice, wslice=wslice:
                       e.tensor_tensor(tmp[:, :], wslice, pslice, ALU.add),
                       deps=[sprev[hi]])
            s = OP("scalar", lambda e, tmp=tmp, wslice=wslice: e.activation(
                wslice, tmp[:, :], AF.Tanh), deps=[DOP(v)])
            sprev[hi] = DOP(s)
            next(kw_iter, None)
    OP("vector", lambda e: e.tensor_copy(sedge[:, :], ws[:, 0:8 * SEG:SEG]),
       deps=[sprev[0], sprev[1]])
    OP("vector", lambda e: e.drain())
    OP("vector", lambda e: e.tensor_tensor(ws[:, :], ws[:, :], segmask[:, :],
                                           ALU.mult), deps=[segm_ld])
    OP("vector", lambda e: e.drain())
    mask_done = DEP("vector")
    sprev = [mask_done, mask_done]
    for jj in range(SEG):
        for hi, (j0, j1) in enumerate(halves):
            tmp = tmp_h[hi]
            cur = pxi[:, jj, j0:j1, 3:3 + SEG]
            if jj == 0:
                prev = wsr[:, j0:j1, :]
            else:
                prev = pxi[:, jj - 1, j0:j1, 3:3 + SEG]
            v = OP("vector", lambda e, tmp=tmp, prev=prev, cur=cur:
                   e.tensor_tensor(tmp[:, :], prev, cur, ALU.add),
                   deps=[sprev[hi]])
            s = OP("scalar", lambda e, tmp=tmp, cur=cur: e.activation(
                cur, tmp[:, :], AF.Tanh), deps=[DOP(v)])
            sprev[hi] = DOP(s)
            next(kw_iter, None)
    OP("vector", lambda e: e.tensor_tensor(
        etmp[:, :], pxi[:, SI - 1, :, SJ - 2:SJ - 1], pxi[:, 0, :, SJ - 1:SJ],
        ALU.add), deps=[sprev[0], sprev[1]])
    OP("scalar", lambda e: e.activation(pxi[:, 0, :, SJ - 1:SJ], etmp[:, :],
                                        AF.Tanh), deps=[DEP("vector")])
    scan_done = DEP("scalar")
    ss.close()
    for _ in kw_iter:
        pass
    skw.close()

    if debug:
        OP("sync", lambda e: e.dma_start(out=dbg_px, in_=px[:, :]), dma="DS",
           deps=[scan_done])
        OP("sync", lambda e: e.dma_start(out=dbg_g, in_=g[:, :]), dma="DS",
           deps=[a_done])

    sc2 = contextlib.ExitStack()
    attnb = [sb("attnb0", [128, 128], BF, sc2), sb("attnb1", [128, 128], BF, sc2)]
    xst2 = [sb("xst2a", [128, D], F32, sc2), sb("xst2b", [128, D], F32, sc2)]
    x2t = [sb("x2ta", [128, D], F32, sc2), sb("x2tb", [128, D], F32, sc2)]
    h2t = sb("h2t", [128, D], F32, sc2)
    ki32 = sb("ki32", [128, D], I32, sc2)
    kxtm = [sb("kxtma", [128, D], BF, sc2), sb("kxtmb", [128, D], BF, sc2)]
    kxst = [sb("kxsta", [128, 8 * 128], BF, sc2),
            sb("kxstb", [128, 8 * 128], BF, sc2)]
    amax_c = sb("amax_c", [128, 1], F32, sc2)
    sct_c = sb("sct_c", [128, 1], F32, sc2)
    isct_c = sb("isct_c", [128, 1], F32, sc2)
    A2 = sb("A2", [128, D], F32, sc2)
    B2 = sb("B2", [128, D], F32, sc2)
    ga_t = sb("ga_t", [128, D], F32, sc2)
    ga_row = sb("ga_row", [1, D], F32, sc2)
    cb_row = sb("cb_row", [1, D], F32, sc2)
    sb_row = sb("sb_row", [1, D], F32, sc2)
    sgE = sb("sgE", [128, 16], F32, sc2)
    keg = sb("keg", [128, 16], BF, sc2)
    kegr = keg.rearrange("p (j c) -> p j c", j=8)
    negm2 = sb("negm2", [128, 1], F32, sc2)
    ssq2 = sb("ssq2", [128, 1], F32, sc2)
    var2 = sb("var2", [128, 1], F32, sc2)
    ist2 = sb("ist2", [128, 1], F32, sc2)
    nm2s = sb("nm2s", [128, 1], F32, sc2)
    t2c = sb("t2c", [128, 1], F32, sc2)
    sq2scr = sb("sq2scr", [128, D], F32, sc2)
    h2p = [h2t, sb("h2tb", [128, D], F32, sc2)]
    ki2p = [ki32, sb("ki32b", [128, D], I32, sc2)]
    amaxp = [amax_c, sb("amaxb", [128, 1], F32, sc2)]
    sctp = [sct_c, sb("sctb", [128, 1], F32, sc2)]
    isctp = [isct_c, sb("isctb", [128, 1], F32, sc2)]
    negmp = [negm2, sb("negm2b", [128, 1], F32, sc2)]
    ssqp = [ssq2, sb("ssq2b", [128, 1], F32, sc2)]
    varp = [var2, sb("var2b", [128, 1], F32, sc2)]
    istp = [ist2, sb("ist2b", [128, 1], F32, sc2)]
    nmsp = [nm2s, sb("nm2sb", [128, 1], F32, sc2)]
    t2p = [t2c, sb("t2cb", [128, 1], F32, sc2)]

    OP("sync", lambda e: e.dma_start(
        out=ga_row[:, :], in_=mods[2 * D:3 * D].rearrange("(a b) -> a b", a=1)),
        dma="DL", deps=[mods_out, scan_done])
    OP("sync", lambda e: e.dma_start(
        out=cb_row[:, :], in_=mods[4 * D:5 * D].rearrange("(a b) -> a b", a=1)),
        dma="DL")
    OP("sync", lambda e: e.dma_start(
        out=sb_row[:, :], in_=mods[3 * D:4 * D].rearrange("(a b) -> a b", a=1)),
        dma="DL")
    rows_in = DMA("DL")
    for base, dst in ((cb_row, A2), (sb_row, B2), (ga_row, ga_t)):
        for h in range(2):
            p = pb0 if h == 0 else pb1
            rhs = base[:, h * 512:(h + 1) * 512]
            t_write(p, lambda e, rhs=rhs, p=p: e.matmul(
                p[:, :], ones_r[:, :], rhs, start=True, stop=True),
                deps=[rows_in])
            consume(p, "vector", lambda e, dst=dst, h=h, p=p: e.tensor_copy(
                dst[:, h * 512:(h + 1) * 512], p[:, :]), deps=[DEP("tensor")])
    OP("vector", lambda e: e.tensor_scalar(A2[:, :], A2[:, :], 1.0, None, ALU.add))
    gate_tile_transform(ga_t[:, :], h2t[:, :])
    bcast_done = DEP("vector")

    ln2_bufs = (negm2[:, :], ssq2[:, :], var2[:, :], ist2[:, :], nm2s[:, :],
                t2c[:, :], sq2scr[:, :])
    ln2e_bufs = tuple(b[0:2, :] for b in ln2_bufs)

    def quant_block(x2_ap, kx_dst, part, bufs, sq_dep, kx_dep=None):
        nd = ln_stats_norm(x2_ap, DEP("vector"), part, h2t[0:part, :], bufs,
                           dst_scalar_dep=sq_dep)
        OP("vector", lambda e, part=part: e.tensor_tensor(
            h2t[0:part, :], h2t[0:part, :], A2[0:part, :], ALU.mult),
            deps=[nd, bcast_done])
        OP("vector", lambda e, part=part: e.tensor_tensor(
            h2t[0:part, :], h2t[0:part, :], B2[0:part, :], ALU.add))
        rr = OP("vector", lambda e, part=part: e.reduce_max(
            amax_c[0:part, :], h2t[0:part, :], axis=AX.X,
            apply_absolute_value=True), deps=[DEP("gpsimd")])
        OP("gpsimd", lambda e, part=part: e.tensor_scalar(
            amax_c[0:part, :], amax_c[0:part, :], EPS, None, ALU.max),
            deps=[DOP(rr)])
        OP("gpsimd", lambda e, part=part: e.tensor_scalar(
            isct_c[0:part, :], amax_c[0:part, :], 1.0 / 127.0, None, ALU.mult))
        OP("vector", lambda e, part=part: e.reciprocal(
            sct_c[0:part, :], amax_c[0:part, :]), deps=[DEP("gpsimd")])
        OP("gpsimd", lambda e, part=part: e.tensor_scalar(
            sct_c[0:part, :], sct_c[0:part, :], 127.0, None, ALU.mult),
            deps=[DEP("vector")])
        gdep = DEP("gpsimd")
        OP("vector", lambda e, part=part: e.tensor_scalar(
            ki32[0:part, :], h2t[0:part, :], sct_c[0:part, 0:1], None, ALU.mult),
            deps=[gdep])
        OP("vector", lambda e, part=part, kx_dst=kx_dst: e.tensor_scalar(
            kx_dst, ki32[0:part, :], isct_c[0:part, 0:1], None, ALU.mult),
            deps=[kx_dep])
        return DEP("vector")

    x2dma = [scan_done, scan_done]
    kxdma = [scan_done, scan_done]
    tp_attn = [None, None]
    gG = [dict(), dict()]
    x2done = [None, None]

    def emit_attn_A_products(sub):
        t0 = sub * 128
        par = sub % 2
        st0 = t0 // SI + 3
        for j in range(8):
            ab = attnb[j % 2]
            gap = gr[:, j, t0 + 1:t0 + 129]
            pap = pxi[:, :, j, st0:st0 + 2].rearrange("p i s -> p s i")
            v = OP("vector", lambda e, ab=ab, gap=gap, pap=pap: e.tensor_tensor(
                ab[:, :], gap, pap, ALU.mult),
                deps=[scan_done, tp_attn[j % 2]])
            p = ptb0 if j < 4 else ptb1
            lw = (ab[:, :], True)
            fn = (lambda e, j=j, p=p, ab=ab: e.transpose(
                p[:, (j % 4) * 128:(j % 4 + 1) * 128], ab[:, :], identb[:, :]))
            if j % 4 == 0:
                i = t_write(p, fn, [DOP(v)], ldw=lw)
            else:
                i = OP("tensor", fn, [DOP(v)], ldw=lw)
            tp_attn[j % 2] = DOP(i)
        OP("sync", lambda e, t0=t0, par=par: e.dma_start(
            out=xst2[par][:, :], in_=xq[WM + t0:WM + t0 + 128, :]), dma="DL",
            deps=[x2dma[par]])
        return DMA("DL")

    def emit_attn_A_rest(sub, xdep):
        t0 = sub * 128
        par = sub % 2
        tdep = DEP("tensor")
        consume(ptb0, "vector", lambda e, par=par: e.tensor_tensor(
            x2t[par][:, 0:512], ptb0[:, 0:512], ga_t[:, 0:512], ALU.mult),
            deps=[tdep, bcast_done, x2dma[par], gG[par].get("norm")])
        consume(ptb1, "vector", lambda e, par=par: e.tensor_tensor(
            x2t[par][:, 512:1024], ptb1[:, 0:512], ga_t[:, 512:1024], ALU.mult))
        a = OP("vector", lambda e, par=par: e.tensor_tensor(
            x2t[par][:, :], x2t[par][:, :], xst2[par][:, :], ALU.add),
            deps=[xdep])
        i = OP("scalar", lambda e, t0=t0, par=par: e.dma_start(
            out=x2s[t0:t0 + 128, :], in_=x2t[par][:, :]), dma="DS",
            deps=[DEP("vector")])
        x2dma[par] = sc.ops[i].dma_tok
        x2done[par] = DOP(a)

    def emit_attn_B_early(sub):
        par = sub % 2
        g_ = gG[par]
        rr = OP("vector", lambda e, par=par: e.reduce_sum(
            negmp[par][:, :], x2t[par][:, :], axis=AX.X, negate=True),
            deps=[x2done[par], g_.get("nms")])
        sq = OP("scalar", lambda e, par=par: e.activation(
            sq2scr[:, :], x2t[par][:, :], AF.Square, accum_out=ssqp[par][:, :]),
            deps=[x2done[par], g_.get("t1r")])
        OP("gpsimd", lambda e, par=par: e.tensor_scalar(
            negmp[par][:, :], negmp[par][:, :], 1.0 / D, None, ALU.mult),
            deps=[DOP(rr)])
        OP("gpsimd", lambda e, par=par: e.tensor_tensor(
            varp[par][:, :], negmp[par][:, :], negmp[par][:, :], ALU.mult))
        t1g = OP("gpsimd", lambda e, par=par: e.tensor_scalar(
            t2p[par][:, :], ssqp[par][:, :], 1.0 / D, None, ALU.mult),
            deps=[DOP(sq), g_.get("recip")])
        OP("gpsimd", lambda e, par=par: e.tensor_tensor(
            varp[par][:, :], t2p[par][:, :], varp[par][:, :], ALU.subtract))
        sqr = OP("scalar", lambda e, par=par: e.activation(
            t2p[par][:, :], varp[par][:, :], AF.Sqrt, bias=eps_c[:, 0:1]),
            deps=[DEP("gpsimd")])
        g_["t1r"] = DOP(t1g)
        g_["sqrt"] = DOP(sqr)

    def emit_attn_B_mid(sub):
        par = sub % 2
        g_ = gG[par]
        rc = OP("vector", lambda e, par=par: e.reciprocal(
            istp[par][:, :], t2p[par][:, :]),
            deps=[g_["sqrt"], g_.get("norm")])
        g_["recip"] = DOP(rc)
        nm = OP("gpsimd", lambda e, par=par: e.tensor_tensor(
            nmsp[par][:, :], negmp[par][:, :], istp[par][:, :], ALU.mult),
            deps=[DOP(rc)])
        g_["nms"] = DOP(nm)
        nrm = OP("scalar", lambda e, par=par: e.activation(
            h2p[par][:, :], x2t[par][:, :], AF.Identity,
            bias=nmsp[par][:, :], scale=istp[par][:, :]),
            deps=[DOP(nm), g_.get("rmax")])
        g_["norm"] = DOP(nrm)
        OP("gpsimd", lambda e, par=par: e.tensor_tensor(
            h2p[par][:, 512:1024], h2p[par][:, 512:1024], A2[:, 512:1024],
            ALU.mult), deps=[DOP(nrm), bcast_done])
        ga2 = OP("gpsimd", lambda e, par=par: e.tensor_tensor(
            h2p[par][:, 512:1024], h2p[par][:, 512:1024], B2[:, 512:1024],
            ALU.add))
        g_["gah1"] = DOP(ga2)

    def emit_attn_B_late2(sub):
        par = sub % 2
        g_ = gG[par]
        OP("vector", lambda e, par=par: e.tensor_tensor(
            h2p[par][:, 0:512], h2p[par][:, 0:512], A2[:, 0:512], ALU.mult),
            deps=[g_["norm"], bcast_done])
        OP("vector", lambda e, par=par: e.tensor_tensor(
            h2p[par][:, 0:512], h2p[par][:, 0:512], B2[:, 0:512], ALU.add))
        rm = OP("vector", lambda e, par=par: e.reduce_max(
            amaxp[par][:, :], h2p[par][:, :], axis=AX.X,
            apply_absolute_value=True), deps=[g_["gah1"], g_.get("isct")])
        g_["rmax"] = DOP(rm)
        OP("gpsimd", lambda e, par=par: e.tensor_scalar(
            amaxp[par][:, :], amaxp[par][:, :], EPS, None, ALU.max),
            deps=[DOP(rm)])
        ic = OP("gpsimd", lambda e, par=par: e.tensor_scalar(
            isctp[par][:, :], amaxp[par][:, :], 1.0 / 127.0, None, ALU.mult),
            deps=[g_.get("kxv")])
        g_["isct"] = DOP(ic)
        rs = OP("vector", lambda e, par=par: e.reciprocal(
            sctp[par][:, :], amaxp[par][:, :]),
            deps=[DEP("gpsimd"), g_.get("ki")])
        sc7 = OP("gpsimd", lambda e, par=par: e.tensor_scalar(
            sctp[par][:, :], sctp[par][:, :], 127.0, None, ALU.mult),
            deps=[DOP(rs)])
        ki_i = OP("scalar", lambda e, par=par: e.activation(
            ki2p[par][:, :], h2p[par][:, :], AF.Identity,
            scale=sctp[par][:, 0:1]), deps=[DOP(sc7), g_.get("kxv")])
        g_["ki"] = DOP(ki_i)
        kxv = OP("vector", lambda e, par=par: e.tensor_scalar(
            kxtm[par][:, :], ki2p[par][:, :], isctp[par][:, 0:1], None,
            ALU.mult), deps=[DOP(ki_i), g_.get("kxtp")])
        g_["kxv"] = DOP(kxv)
        qdep = DOP(kxv)
        last_tp = None
        for ph in range(2):
            p = ptb0 if ph == 0 else ptb1
            for jj in range(4):
                j = ph * 4 + jj
                lw = (kxtm[par][:, j * 128:(j + 1) * 128], True)
                fn = (lambda e, j=j, jj=jj, p=p, par=par: e.transpose(
                    p[:, jj * 128:(jj + 1) * 128],
                    kxtm[par][:, j * 128:(j + 1) * 128], identb[:, :]))
                if jj == 0:
                    i = t_write(p, fn, [qdep], ldw=lw)
                else:
                    i = t_acc(fn, ldw=lw)
            last_tp = i
            consume(p, "vector", lambda e, ph=ph, p=p, par=par: e.tensor_copy(
                kxst[par].rearrange("p (j c) -> p j c", j=8)[:, ph * 4:(ph + 1) * 4, :],
                p[:, 0:512]),
                deps=[DEP("tensor"), kxdma[par]])
        g_["kxtp"] = DOP(last_tp)
        t0 = sub * 128
        i = OP("scalar", lambda e, t0=t0, par=par: e.dma_start(
            out=kxsr[:, :, t0 + 1:t0 + 129],
            in_=kxst[par].rearrange("p (j c) -> p j c", j=8)), dma="DS",
            deps=[DEP("vector")])
        kxdma[par] = sc.ops[i].dma_tok

    for sub in range(32):
        if sub >= 1:
            emit_attn_B_early(sub - 1)
        xdep = emit_attn_A_products(sub)
        if sub >= 1:
            emit_attn_B_mid(sub - 1)
        emit_attn_A_rest(sub, xdep)
        if sub >= 1:
            emit_attn_B_late2(sub - 1)
    emit_attn_B_early(31)
    emit_attn_B_mid(31)
    emit_attn_B_late2(31)

    for j in range(8):
        OP("vector", lambda e, j=j: e.tensor_tensor(
            sgE[:, 2 * j:2 * j + 1], gr[:, j, 0:1], sedge[:, j:j + 1], ALU.mult),
            deps=[scan_done])
        OP("vector", lambda e, j=j: e.tensor_tensor(
            sgE[:, 2 * j + 1:2 * j + 2], gr[:, j, GW - 1:GW],
            pxi[:, 0, j, SJ - 1:SJ], ALU.mult))
    OP("vector", lambda e: e.drain())
    sg_dep = DEP("vector")
    OP("sync", lambda e: e.dma_start(
        out=xst2[0][0:2, :], in_=xq[WM - 1:WM + T + 1:T + 1, :]), dma="DL",
        deps=[x2dma[0]])
    xe_dep = DMA("DL")
    for j in range(8):
        p = pt0 if j < 4 else pt1
        fn = (lambda e, j=j, p=p: e.transpose(
            p[0:2, (j % 4) * 128:(j % 4 + 1) * 128], sgE[:, 2 * j:2 * j + 2],
            ident32[:, :]))
        if j % 4 == 0:
            t_write(p, fn, [sg_dep])
        else:
            t_acc(fn)
    tdep = DEP("tensor")
    consume(pt0, "vector", lambda e: e.tensor_tensor(
        x2t[0][0:2, 0:512], pt0[0:2, :], ga_t[0:2, 0:512], ALU.mult),
        deps=[tdep, bcast_done, x2dma[0], gG[0].get("norm")])
    consume(pt1, "vector", lambda e: e.tensor_tensor(
        x2t[0][0:2, 512:1024], pt1[0:2, :], ga_t[0:2, 512:1024], ALU.mult))
    OP("vector", lambda e: e.tensor_tensor(x2t[0][0:2, :], x2t[0][0:2, :],
                                           xst2[0][0:2, :], ALU.add),
       deps=[xe_dep])
    quant_block(x2t[0][0:2, :], kxtm[0][0:2, :], 2, ln2e_bufs, DEP("scalar"),
                kx_dep=gG[0].get("kxtp"))
    OP("vector", lambda e: e.tensor_scalar(kxtm[0][0:2, :], kxtm[0][0:2, :],
                                           edgemask[0:2, 0:1], None, ALU.mult),
       deps=[ld0])
    qdep = DEP("vector")
    for j in range(8):
        p = ptb0 if j < 4 else ptb1
        lw = (kxtm[0][0:2, j * 128:(j + 1) * 128], True)
        fn = (lambda e, j=j, p=p: e.transpose(
            p[:, (j % 4) * 2:(j % 4) * 2 + 2], kxtm[0][0:2, j * 128:(j + 1) * 128],
            identb[0:2, 0:2]))
        if j % 4 == 0:
            t_write(p, fn, [qdep], ldw=lw)
        else:
            t_acc(fn, ldw=lw)
    tdep = DEP("tensor")
    consume(ptb0, "vector", lambda e: e.tensor_copy(kegr[:, 0:4, :], ptb0[:, 0:8]),
            deps=[tdep])
    consume(ptb1, "vector", lambda e: e.tensor_copy(kegr[:, 4:8, :],
                                                    ptb1[:, 0:8]))
    OP("sync", lambda e: e.dma_start(out=kxsr[:, :, 0:1], in_=kegr[:, :, 0:1],
                                     allow_slow_non_contiguous=True),
       dma="DS", deps=[DEP("vector")])
    OP("sync", lambda e: e.dma_start(out=kxsr[:, :, GW - 1:GW],
                                     in_=kegr[:, :, 1:2],
                                     allow_slow_non_contiguous=True), dma="DS")

    sc2.close()
    s1.close()
    attn_barrier = sc.barrier_deps()

    s2 = contextlib.ExitStack()
    kw1t = sb("kw1t", [128, 8 * DH], BF, s2)
    kw2t = sb("kw2t", [128, 32 * D], BF, s2)

    for c8 in range(8):
        OP("sync", lambda e, c8=c8: e.dma_start(
            out=kw1t[:, c8 * DH:(c8 + 1) * DH], in_=kw1d[:, c8 * DH:(c8 + 1) * DH]),
            dma="DL",
            deps=(list(attn_barrier) + [sc.ddma("DG")]) if c8 == 0 else [])
    kw1_ld = DMA("DL")
    for c8 in range(8):
        OP("sync", lambda e, c8=c8: e.dma_start(
            out=kw2t[:, c8 * 4 * D:(c8 + 1) * 4 * D],
            in_=kw2d[:, c8 * 4 * D:(c8 + 1) * 4 * D]), dma="DL")
    kw2_ld = DMA("DL")

    y = [sb("y0", [128, 32 * 258], BF, s2), sb("y1", [128, 32 * 258], BF, s2)]
    yr = [t.rearrange("p (h c) -> p h c", h=32) for t in y]
    kht = sb("kht", [128, 32 * MT], BF, s2)
    khtr = kht.rearrange("p (h c) -> p h c", h=32)
    kxt = sb("kxt0", [128, 8 * 258], BF, s2)
    kxtr = kxt.rearrange("p (j c) -> p j c", j=8)
    x2rt = sb("x2rt", [128, D], F32, s2)
    NCONV = 4
    convo = [sb(f"convo{v}", [128, MT], BF, s2) for v in range(NCONV)]
    mmax = sb("mmax", [128, MT], BF, s2)
    mmin = sb("mmin", [128, MT], BF, s2)
    khi = sb("khi", [128, MT], I32, s2)
    mcol = sb("mcol", [128, 1], F32, s2)
    rcol = sb("rcol", [128, 1], F32, s2)
    s2c = sb("s2c", [128, 1], F32, s2)
    i2cc = sb("i2cc", [128, 2], F32, s2)
    rowbuf = sb("rowbuf", [1, MT], F32, s2)
    otb = sb("otb", [128, 8 * MT], BF, s2)
    outt = [sb("outt0", [128, D], F32, s2), sb("outt1", [128, D], F32, s2)]

    kx_ready = sc.ddma("DS")
    do_tok = [None, None]
    xr_guard = [None, None]
    conv_reader = [[None] * 32, [None] * 32]
    w1_mm_last = [None, None]
    kht_reader = [None]
    mm_reader = [None]
    silu_ops = [None] * 32
    silu_hist = [None] * NCONV

    def emit_w1_group(m, h, wdep):
        buf = m % 2
        p = pm0 if h % 2 == 0 else pm1
        last = None
        for j in range(8):
            lw = (kw1t[:, j * DH + h * 128:j * DH + (h + 1) * 128], False)
            fn = (lambda e, h=h, j=j, p=p, buf=buf: e.matmul(
                p[:, 0:258], kw1t[:, j * DH + h * 128:j * DH + (h + 1) * 128],
                kxtr[:, j, :], start=(j == 0), stop=(j == 7)))
            if j == 0:
                last = t_write(p, fn, [wdep, kw1_ld], ldw=lw)
            else:
                last = t_acc(fn, ldw=lw)
        consume(p, "scalar", lambda e, h=h, p=p, buf=buf: e.activation(
            yr[buf][:, h, :], p[:, 0:258], AF.Identity,
            scale=swc1[:, 0:1]), deps=[DEP("tensor"), conv_reader[buf][h]])
        return last, DEP("scalar")

    def emit_conv(mm, h, evac_dep):
        buf = mm % 2
        cv_ = convo[h % NCONV]
        OP("vector", lambda e, h=h, buf=buf, cv_=cv_: e.tensor_scalar(
            cv_[:, :], yr[buf][:, h, 0:256], cw_c[:, 3 * h:3 * h + 1],
            None, ALU.mult), deps=[evac_dep, silu_hist[h % NCONV]])
        OP("vector", lambda e, h=h, buf=buf, cv_=cv_: e.scalar_tensor_tensor(
            cv_[:, :], yr[buf][:, h, 1:257], cw_c[:, 3 * h + 1:3 * h + 2],
            cv_[:, :], ALU.mult, ALU.add))
        i = OP("vector", lambda e, h=h, buf=buf, cv_=cv_: e.scalar_tensor_tensor(
            cv_[:, :], yr[buf][:, h, 2:258], cw_c[:, 3 * h + 2:3 * h + 3],
            cv_[:, :], ALU.mult, ALU.add))
        conv_reader[buf][h] = DOP(i)
        si = OP("scalar", lambda e, h=h, cv_=cv_: e.activation(
            khtr[:, h, :], cv_[:, :], AF.Silu, bias=cb_c[:, h:h + 1]),
            deps=[DOP(i), kht_reader[0]])
        silu_hist[h % NCONV] = DOP(si)
        silu_ops[h] = DOP(si)

    def emit_maxmin(h):
        OP("vector", lambda e, h=h: e.tensor_tensor(
            mmax[:, :], mmax[:, :], khtr[:, h, :], ALU.max),
            deps=[silu_ops[h]])
        OP("vector", lambda e, h=h: e.tensor_tensor(
            mmin[:, :], mmin[:, :], khtr[:, h, :], ALU.min))

    def merged_tile(m, evac_prev):
        have_w1 = m < NMT
        have_conv = m >= 1
        evac_cur = [None] * 32
        wdep = None
        if have_w1:
            buf = m % 2
            OP("sync", lambda e, m=m, buf=buf: e.dma_start(
                out=kxtr[:, :, :], in_=kxsr[:, :, m * MT:m * MT + 258]),
                dma="DL", deps=[kx_ready, w1_mm_last[0]])
            wdep = DMA("DL")
        if have_conv:
            OP("vector", lambda e: e.memset(mmax[:, :], 0.0),
               deps=[mm_reader[0]])
            OP("vector", lambda e: e.memset(mmin[:, :], 0.0))
        last_mm = None
        for h in range(32):
            if have_w1:
                last_mm, ev = emit_w1_group(m, h, wdep)
                evac_cur[h] = ev
            if have_conv:
                emit_conv(m - 1, h, evac_prev[h])
                if h >= 1:
                    emit_maxmin(h - 1)
        if have_w1:
            w1_mm_last[0] = DOP(last_mm)
        if have_conv:
            emit_maxmin(31)
        return evac_cur

    def mid_quant(m):
        OP("vector", lambda e: e.tensor_scalar(mmin[:, :], mmin[:, :], -1.0,
                                               None, ALU.mult))
        OP("vector", lambda e: e.tensor_tensor(mmax[:, :], mmax[:, :],
                                               mmin[:, :], ALU.max))
        OP("vector", lambda e: e.drain())
        first_tp = True
        for u in range(2):
            pT = ptb0 if u == 0 else ptb1
            mm_reader[0] = DOP(t_write(pT, lambda e, u=u, pT=pT: e.transpose(
                pT[:, 0:128], mmax[:, u * 128:(u + 1) * 128], identb[:, :]),
                deps=[DEP("vector")],
                ldw=(mmax[:, u * 128:(u + 1) * 128], True)))
            rr = consume(pT, "vector", lambda e, pT=pT: e.reduce_max(
                mcol[:, :], pT[:, 0:128], axis=AX.X),
                deps=[DEP("tensor"), DEP("gpsimd")])
            OP("gpsimd", lambda e: e.tensor_scalar(mcol[:, :], mcol[:, :], EPS,
                                                   None, ALU.max),
               deps=[DOP(rr)])
            OP("gpsimd", lambda e, u=u: e.tensor_scalar(
                i2cc[:, u:u + 1], mcol[:, :], 1.0 / 127.0, None, ALU.mult))
            OP("gpsimd", lambda e, u=u: e.tensor_tensor(
                i2cc[:, u:u + 1], i2cc[:, u:u + 1], swc2[:, 0:1], ALU.mult))
            OP("vector", lambda e: e.reciprocal(rcol[:, :], mcol[:, :]),
               deps=[DEP("gpsimd")])
            OP("gpsimd", lambda e: e.tensor_scalar(s2c[:, :], rcol[:, :], 127.0,
                                                   None, ALU.mult),
               deps=[DEP("vector")])
            vdep = DEP("gpsimd")
            fn1 = (lambda e, u=u: e.transpose(
                pb0[0:1, u * 128:(u + 1) * 128], s2c[:, :], ident32[:, :]))
            if first_tp:
                t_write(pb0, fn1, [vdep])
                first_tp = False
            else:
                OP("tensor", fn1, [vdep])
        consume(pb0, "vector", lambda e: e.tensor_copy(
            rowbuf[:, :], pb0[0:1, 0:MT]), deps=[DEP("tensor")])
        OP("vector", lambda e: e.drain())
        rdep = DEP("vector")
        t_write(pb0, lambda e: e.matmul(pb0[:, 0:MT], ones_r[:, :],
                                        rowbuf[:, 0:MT],
                                        start=True, stop=True), deps=[rdep])
        s2bc_t = DEP("tensor")
        qdeps = []
        for h in range(32):
            OP("vector", lambda e, h=h: e.tensor_tensor(
                khi[:, :], khtr[:, h, :], pb0[:, 0:MT], ALU.mult),
                deps=[s2bc_t])
            q = OP("vector", lambda e, h=h: e.tensor_copy(
                khtr[:, h, :], khi[:, :]))
            qdeps.append(DOP(q))
        bank_guard[id(pb0)] = [qdeps[-1]]
        return qdeps

    def w2_out_tile(m, qdeps):
        for k in range(8):
            p = pm0 if k % 2 == 0 else pm1
            for hh in range(32):
                lw = (kw2t[:, hh * D + k * 128:hh * D + (k + 1) * 128], False)
                fn = (lambda e, k=k, hh=hh, p=p: e.matmul(
                    p[:, 0:MT], kw2t[:, hh * D + k * 128:hh * D + (k + 1) * 128],
                    khtr[:, hh, :], start=(hh == 0), stop=(hh == 31)))
                if hh == 0:
                    i = t_write(p, fn, ([qdeps[0], kw2_ld] if k == 0 else []),
                                ldw=lw)
                elif k == 0:
                    i = OP("tensor", fn, [qdeps[hh]], ldw=lw)
                else:
                    i = t_acc(fn, ldw=lw)
            last_w2 = i
            consume(p, "scalar", lambda e, k=k, p=p: e.activation(
                otb[:, k * MT:(k + 1) * MT], p[:, 0:MT], AF.Identity),
                deps=[DEP("tensor")])
        odep = DEP("scalar")
        kreader = DOP(last_w2)
        for u in range(2):
            xr = x2rt
            ot_ = outt[u]
            OP("sync", lambda e, m=m, u=u, xr=xr: e.dma_start(
                out=xr[:, :],
                in_=x2s[m * MT + u * 128:m * MT + (u + 1) * 128, :]),
                dma="DL", deps=[xr_guard[0], kx_ready])
            x2dep = DMA("DL")
            for ph in range(2):
                p = ptb0 if ph == 0 else ptb1
                for kk in range(4):
                    k = ph * 4 + kk
                    lw = (otb[:, k * MT + u * 128:k * MT + (u + 1) * 128], True)
                    fn = (lambda e, k=k, kk=kk, u=u, p=p: e.transpose(
                        p[:, kk * 128:(kk + 1) * 128],
                        otb[:, k * MT + u * 128:k * MT + (u + 1) * 128],
                        identb[:, :]))
                    if kk == 0:
                        t_write(p, fn, [odep], ldw=lw)
                    else:
                        t_acc(fn, ldw=lw)
            tdep = DEP("tensor")
            consume(ptb0, "vector", lambda e, u=u, ot_=ot_: e.scalar_tensor_tensor(
                ot_[:, 0:512], ptb0[:, 0:512], i2cc[:, u:u + 1],
                gb_t[:, 0:512], ALU.mult, ALU.mult),
                deps=[tdep, do_tok[u]])
            consume(ptb1, "vector", lambda e, u=u, ot_=ot_: e.scalar_tensor_tensor(
                ot_[:, 512:1024], ptb1[:, 0:512], i2cc[:, u:u + 1],
                gb_t[:, 512:1024], ALU.mult, ALU.mult))
            a = OP("vector", lambda e, xr=xr, ot_=ot_: e.tensor_tensor(
                ot_[:, :], ot_[:, :], xr[:, :], ALU.add), deps=[x2dep])
            xr_guard[0] = DOP(a)
            i = OP("scalar", lambda e, m=m, u=u, ot_=ot_: e.dma_start(
                out=out_d[m * MT + u * 128:m * MT + (u + 1) * 128, :],
                in_=ot_[:, :]), dma="DO", deps=[DEP("vector")])
            do_tok[u] = sc.ops[i].dma_tok
        return kreader

    evac_prev = [None] * 32
    for m in range(NMT + 1):
        evac_prev = merged_tile(m, evac_prev)
        if m >= 1:
            qd = mid_quant(m - 1)
            kht_reader[0] = w2_out_tile(m - 1, qd)
        if debug and m == 2:
            OP("sync", lambda e: e.dma_start(out=dbg_y, in_=y[1][:, :]), dma="DS",
               deps=[DEP("vector")])
    s2.close()

    OP("sync", None, deps=[sc.ddma("DO"), sc.ddma("DS"), sc.ddma("DL"),
                           sc.ddma("DG")])

    with nc.Block() as blk:
        sc.emit(blk)
    return nc


_NC_CACHE = {}


def _get_nc(debug=False):
    key = bool(debug)
    if key not in _NC_CACHE:
        _NC_CACHE[key] = build_nc(debug)
    return _NC_CACHE[key]


def make_in_maps(inputs):
    x = np.asarray(inputs["L0_attn_x"], np.float32)
    c = np.asarray(inputs["L0_c"], np.float32)
    identb = np.eye(128, dtype=ml_dtypes.bfloat16)
    ident32 = np.eye(128, dtype=np.float32)
    cwr = np.ascontiguousarray(np.asarray(inputs["L0_ffn_conv_w"],
                                          np.float32).reshape(DH, 3))
    common = {
        "wst": np.ascontiguousarray(inputs["L0_attn_W_state"], dtype=np.float32),
        "wgt": np.ascontiguousarray(inputs["L0_attn_W_gate"], dtype=np.float32),
        "bst_in": np.ascontiguousarray(inputs["L0_attn_b_state"], dtype=np.float32),
        "bgt_in": np.ascontiguousarray(inputs["L0_attn_b_gate"], dtype=np.float32),
        "w1": np.ascontiguousarray(inputs["L0_ffn_w1"], dtype=np.float32),
        "cw": cwr,
        "cbv": np.ascontiguousarray(inputs["L0_ffn_conv_b"], dtype=np.float32),
        "w2": np.ascontiguousarray(inputs["L0_ffn_w2"], dtype=np.float32),
        "wmod": np.ascontiguousarray(inputs["L0_wmod"], dtype=np.float32),
        "identb_in": identb,
        "ident32_in": ident32,
    }
    in_maps = []
    for core in range(8):
        b, q = core // 4, core % 4
        t0 = q * T - WM
        xqv = np.zeros((XR, D), np.float32)
        lo, hi = max(t0, 0), min(t0 + XR, 16384)
        xqv[lo - t0:hi - t0] = x[b, lo:hi]
        segmask = np.ones((128, 8 * SEG), np.float16)
        if q == 0:
            segmask[:, 0::SEG] = 0.0
        edgemask = np.ones((2, 1), np.float32)
        if q == 0:
            edgemask[0, 0] = 0.0
        if q == 3:
            edgemask[1, 0] = 0.0
        m = dict(common)
        m["xq"] = xqv
        m["cvec"] = np.ascontiguousarray(c[b])
        m["segmask_in"] = segmask
        m["edgemask_in"] = edgemask
        in_maps.append(m)
    return in_maps


def kernel(**inputs):
    nc = _get_nc(debug=False)
    in_maps = make_in_maps(inputs)
    res = run_bass_kernel_spmd(nc, in_maps, core_ids=list(range(8)))
    out = np.empty((2, 16384, D), np.float32)
    for core in range(8):
        b, q = core // 4, core % 4
        out[b, q * T:(q + 1) * T] = res.results[core]["out"]
    return out


# version 14
# speedup vs baseline: 1.0133x; 1.0133x over previous
import contextlib
import numpy as np
import ml_dtypes
import concourse.bass as bass
import concourse.mybir as mybir
from concourse.bass_utils import run_bass_kernel_spmd

F32 = mybir.dt.float32
F16 = mybir.dt.float16
BF = mybir.dt.bfloat16
I32 = mybir.dt.int32
AF = mybir.ActivationFunctionType
ALU = mybir.AluOpType
AX = mybir.AxisListType

D = 1024
DH = 4096
T = 4096
WM = 192
XR = 4352
SI = 64
SJ = 68
SEG = 64
PW = SI * 8 * SJ // 8
GW = T + 2
MT = 256
NMT = T // MT
EPS = 1e-5


class _Op:
    __slots__ = ("eng", "fn", "waits", "incs", "sync_val", "dma_tok", "ldw")

    def __init__(self, eng, fn, waits):
        self.eng, self.fn, self.waits = eng, fn, waits
        self.incs = []
        self.sync_val = None
        self.dma_tok = None
        self.ldw = None


class Sched:
    ENGS = ("sync", "tensor", "vector", "scalar", "gpsimd")

    def __init__(self, nc, es):
        self.nc, self.es = nc, es
        self.ops = []
        self.last = {e: None for e in self.ENGS}
        self.sync_count = {e: 0 for e in self.ENGS}
        self.dma_count = {}
        self.sems = {}

    def _sem(self, name):
        if name not in self.sems:
            self.sems[name] = self.es.enter_context(self.nc.semaphore(name))
        return self.sems[name]

    def op(self, eng, fn, deps=(), dma=None, ldw=None):
        waits = [t for t in deps if t is not None]
        o = _Op(eng, fn, waits)
        o.ldw = ldw
        self.ops.append(o)
        self.last[eng] = len(self.ops) - 1
        if dma is not None:
            self.dma_count[dma] = self.dma_count.get(dma, 0) + 16
            o.incs.append((dma, 16))
            o.dma_tok = (dma, self.dma_count[dma])
        return len(self.ops) - 1

    def dop(self, idx):
        o = self.ops[idx]
        if o.dma_tok is not None:
            return o.dma_tok
        o.sync_val = -1
        return ("OPREF", idx)

    def d(self, eng):
        li = self.last[eng]
        return None if li is None else self.dop(li)

    def ddma(self, name):
        c = self.dma_count.get(name, 0)
        return None if c == 0 else (name, c)

    def barrier_deps(self):
        return [self.d(e) for e in self.ENGS if self.last[e] is not None] + \
               [self.ddma(n) for n in self.dma_count]

    def emit(self, blk):
        for n in list(self.dma_count):
            self._sem(n)
        for e in self.ENGS:
            self._sem("E_" + e)
        counts = {e: 0 for e in self.ENGS}
        for i, o in enumerate(self.ops):
            if o.sync_val is not None:
                counts[o.eng] += 1
                o.sync_val = counts[o.eng]
                o.incs.append(("E_" + o.eng, 1))
        for o in self.ops:
            rw = []
            for sn, val in o.waits:
                if sn == "OPREF":
                    t = self.ops[val]
                    rw.append(("E_" + t.eng, t.sync_val))
                else:
                    rw.append((sn, val))
            o.waits = rw
        eng_ops = {e: [o for o in self.ops if o.eng == e] for e in self.ENGS}

        def runner(ename):
            def run(eng):
                waited = {}
                for o in eng_ops[ename]:
                    for sn, val in o.waits:
                        if waited.get(sn, -1) >= val:
                            continue
                        eng.wait_ge(self.sems[sn], val)
                        waited[sn] = val
                    if o.fn is None:
                        continue
                    if o.ldw is not None:
                        ap, istr = o.ldw
                        if istr:
                            eng.ldweights(ap, is_transpose=True)
                        else:
                            eng.ldweights(ap)
                    inst = o.fn(eng)
                    for sn, n in o.incs:
                        inst.then_inc(self.sems[sn], n)
            return run

        blk.sync(runner("sync"))
        blk.tensor(runner("tensor"))
        blk.vector(runner("vector"))
        blk.scalar(runner("scalar"))
        blk.gpsimd(runner("gpsimd"))


def build_nc(debug=False):
    nc = bass.Bass("TRN2", target_bir_lowering=False)
    es = contextlib.ExitStack()
    nc._keepalive = es

    def din(name, shape, dt):
        return nc.dram_tensor(name, shape, dt, kind="ExternalInput").ap()

    xq = din("xq", [XR, D], F32)
    wst = din("wst", [D, D], F32)
    wgt = din("wgt", [D, D], F32)
    bst_in = din("bst_in", [D], F32)
    bgt_in = din("bgt_in", [D], F32)
    w1 = din("w1", [DH, D], F32)
    cw = din("cw", [DH, 3], F32)
    cbv = din("cbv", [DH], F32)
    w2 = din("w2", [D, DH], F32)
    cvec = din("cvec", [D], F32)
    wmod = din("wmod", [D, 6 * D], F32)
    identb_in = din("identb_in", [128, 128], BF)
    ident32_in = din("ident32_in", [128, 128], F32)
    segmask_in = din("segmask_in", [128, 8 * SEG], F16)
    edgemask_in = din("edgemask_in", [2, 1], F32)

    out_d = nc.dram_tensor("out", [T, D], F32, kind="ExternalOutput").ap()
    mods = nc.dram_tensor("mods", [6 * D], F32, kind="Internal").ap()
    x2s = nc.dram_tensor("x2s", [T, D], F32,
                         kind="ExternalOutput" if debug else "Internal").ap()
    kxs = nc.dram_tensor("kxs", [8, 128, GW], BF,
                         kind="ExternalOutput" if debug else "Internal").ap()
    kxsr = kxs.rearrange("j p c -> p j c")
    if debug:
        dbg_px = nc.dram_tensor("dbg_px", [128, 8 * PW], F16,
                                kind="ExternalOutput").ap()
        dbg_g = nc.dram_tensor("dbg_g", [128, 8 * GW], F16,
                               kind="ExternalOutput").ap()
        dbg_y = nc.dram_tensor("dbg_y", [128, 32 * 258], BF,
                               kind="ExternalOutput").ap()
        dbg_xst = nc.dram_tensor("dbg_xst", [128, 2 * 4 * D], BF,
                                 kind="ExternalOutput").ap()
        dbg_xnf = nc.dram_tensor("dbg_xnf", [128, 2 * 8 * 512], BF,
                                 kind="ExternalOutput").ap()

    sc = Sched(nc, es)
    OP, DEP, DOP, DMA = sc.op, sc.d, sc.dop, sc.ddma

    def sb(name, shape, dt, stack, side="left"):
        return stack.enter_context(nc.sbuf_tensor(name, shape, dt, side=side))

    identb = sb("identb", [128, 128], BF, es)
    ident32 = sb("ident32", [128, 128], F32, es)
    ones_r = sb("ones_r", [1, 128], F32, es)
    one_c = sb("one_c", [128, 1], F32, es)
    eps_c = sb("eps_c", [128, 1], F32, es)
    cvec_c = sb("cvec_c", [128, 8], F32, es)
    silc_c = sb("silc_c", [128, 8], F32, es)
    capca_c = sb("capca_c", [128, 8], F32, es)
    uvec_c = sb("uvec_c", [128, 8], BF, es)
    uvec_f = sb("uvec_f", [128, 8], F32, es)
    ca_col = sb("ca_col", [128, 8], F32, es)
    sa_col = sb("sa_col", [128, 8], F32, es)
    bst_c = sb("bst_c", [128, 8], F32, es)
    bgt_c = sb("bgt_c", [128, 8], F32, es)
    cw_c = sb("cw_c", [128, 96], F32, es)
    cb_c = sb("cb_c", [128, 32], F32, es)
    swc1 = sb("swc1", [128, 1], F32, es)
    sq1_c = sb("sq1_c", [128, 1], F32, es)
    sq2_c = sb("sq2_c", [128, 1], F32, es)
    swc2 = sb("swc2", [128, 1], F32, es)
    edgemask = sb("edgemask", [2, 1], F32, es)
    gb_t = sb("gb_t", [128, D], F32, es, side="right")
    sw_r = sb("sw_r", [1, 1], F32, es)
    acc_c = sb("acc_c", [128, 1], F32, es)
    sedge = sb("sedge", [128, 8], F32, es)
    red_c = sb("red_c", [128, 1], F32, es)

    pm0 = es.enter_context(nc.psum_tensor("pm0", [128, 512], F32))
    pm1 = es.enter_context(nc.psum_tensor("pm1", [128, 512], F32))
    pt0 = es.enter_context(nc.psum_tensor("pt0", [128, 512], F32))
    pt1 = es.enter_context(nc.psum_tensor("pt1", [128, 512], F32))
    pb0 = es.enter_context(nc.psum_tensor("pb0", [128, 512], F32))
    pb1 = es.enter_context(nc.psum_tensor("pb1", [128, 512], F32))
    ptb0 = es.enter_context(nc.psum_tensor("ptb0", [128, 1024], BF))
    ptb1 = es.enter_context(nc.psum_tensor("ptb1", [128, 1024], BF))
    bank_guard = {id(p): [] for p in (pm0, pm1, pt0, pt1, pb0, pb1, ptb0, ptb1)}

    def t_write(p, fn, deps=(), ldw=None):
        return OP("tensor", fn, list(deps) + bank_guard[id(p)], ldw=ldw)

    def t_acc(fn, ldw=None):
        return OP("tensor", fn, ldw=ldw)

    def consume(p, eng, fn, deps=()):
        i = OP(eng, fn, deps)
        bank_guard[id(p)] = [DOP(i)]
        return i

    s1 = contextlib.ExitStack()
    px = sb("px", [128, SI * 8 * SJ], F16, s1)
    g = sb("g", [128, 8 * GW], F16, s1)
    pxi = px.rearrange("p (i j s) -> p i j s", i=SI, j=8)
    gr = g.rearrange("p (j c) -> p j c", j=8)

    OP("sync", lambda e: e.dma_start(out=identb[:, :], in_=identb_in), dma="DL")
    OP("sync", lambda e: e.dma_start(out=ident32[:, :], in_=ident32_in), dma="DL")
    OP("sync", lambda e: e.dma_start(out=edgemask[:, :], in_=edgemask_in), dma="DL")
    OP("sync", lambda e: e.dma_start(
        out=cvec_c[:, :], in_=cvec.rearrange("(f p) -> p f", p=128), allow_slow_non_contiguous=True), dma="DL")
    OP("sync", lambda e: e.dma_start(
        out=cw_c.rearrange("p (f k) -> p f k", k=3), in_=cw.rearrange("(f p) k -> p f k", p=128), allow_slow_non_contiguous=True), dma="DL")
    OP("sync", lambda e: e.dma_start(
        out=cb_c[:, :], in_=cbv.rearrange("(f p) -> p f", p=128), allow_slow_non_contiguous=True), dma="DL")
    ld0 = DMA("DL")
    OP("vector", lambda e: e.memset(ones_r[:, :], 1.0))
    OP("vector", lambda e: e.memset(one_c[:, :], 1.0))
    OP("vector", lambda e: e.memset(eps_c[:, :], EPS))
    OP("scalar", lambda e: e.activation(silc_c[:, :], cvec_c[:, :], AF.Silu),
       deps=[ld0])
    sil_done = DEP("scalar")

    smod = contextlib.ExitStack()
    wmst = [sb("wmst0", [128, 3 * D // 2], F32, smod, side="right"),
            sb("wmst1", [128, 3 * D // 2], F32, smod, side="right")]
    modc = sb("modc", [128, 48], F32, smod, side="right")
    modr2 = sb("modr2", [48, 128], F32, smod, side="right")
    gbrow = sb("gbrow", [1, D], F32, smod, side="right")

    s1w = contextlib.ExitStack()
    wts = sb("wts", [128, 8 * D], BF, s1w)
    wtg = sb("wtg", [128, 8 * D], BF, s1w)
    sw = contextlib.ExitStack()
    wrow = [sb("wrow0", [128, D], F32, sw, side="right"),
            sb("wrow1", [128, D], F32, sw, side="right")]

    mm_guard = [None, None]
    tp_guard = [None, None]
    first_mod = [True]

    def mod_piece(gi, j, qh):
        par = gi % 2
        OP("sync", lambda e, j=j, qh=qh, par=par: e.dma_start(
            out=wmst[par][:, :],
            in_=wmod[j * 128:(j + 1) * 128,
                     qh * 3 * D // 2:(qh + 1) * 3 * D // 2]),
            dma="DL", deps=[mm_guard[par]])
        wdep = DMA("DL")
        last = None
        for cc in range(12):
            ch = qh * 12 + cc
            fn = (lambda e, cc=cc, ch=ch, j=j, par=par: e.matmul(
                pb0[:, (ch * 8 + j):(ch * 8 + j) + 1],
                wmst[par][:, cc * 128:(cc + 1) * 128],
                silc_c[:, j:j + 1], start=True, stop=True))
            if first_mod[0]:
                last = t_write(pb0, fn, [wdep, sil_done])
                first_mod[0] = False
            elif cc == 0:
                last = OP("tensor", fn, [wdep, sil_done])
            else:
                last = t_acc(fn)
        mm_guard[par] = DOP(last)

    def wt_piece(gi2, wsrc, dst, i):
        par = gi2 % 2
        stg = wrow[par]
        OP("sync", lambda e, i=i, stg=stg, wsrc=wsrc: e.dma_start(
            out=stg[:, :], in_=wsrc[i * 128:(i + 1) * 128, :]), dma="DL",
            deps=[tp_guard[par]])
        wdep = DMA("DL")
        last = None
        for j in range(8):
            p = pt0 if j % 2 == 0 else pt1
            last = t_write(p, lambda e, j=j, p=p, stg=stg: e.transpose(
                p[:, 0:128], stg[:, j * 128:(j + 1) * 128], ident32[:, :]),
                deps=[wdep, ld0])
            consume(p, "vector",
                    lambda e, i=i, j=j, p=p, dst=dst: e.tensor_copy(
                        dst[:, j * D + i * 128:j * D + (i + 1) * 128],
                        p[:, 0:128]),
                    deps=[DEP("tensor")])
        tp_guard[par] = DOP(last)

    wt_jobs = [(wsrc, dst, i) for wsrc, dst in ((wst, wts), (wgt, wtg))
               for i in range(8)]
    gi = 0
    gi2 = 0
    for j in range(8):
        for qh in range(4):
            mod_piece(gi, j, qh)
            gi += 1
            if gi % 2 == 0 and gi2 < 16:
                wt_piece(gi2, *wt_jobs[gi2])
                gi2 += 1
    while gi2 < 16:
        wt_piece(gi2, *wt_jobs[gi2])
        gi2 += 1
    sw.close()
    wt_done = DEP("vector")

    consume(pb0, "vector", lambda e: e.reduce_sum(
        modc[:, :], pb0[:, 0:384].rearrange("p (c j) -> p c j", j=8),
        axis=AX.X), deps=[DEP("tensor")])
    OP("vector", lambda e: e.drain())
    t_write(pt0, lambda e: e.transpose(pt0[0:48, 0:128], modc[:, 0:48],
                                       ident32[:, :]), deps=[DEP("vector")])
    consume(pt0, "vector", lambda e: e.tensor_copy(modr2[:, :],
                                                   pt0[0:48, 0:128]),
            deps=[DEP("tensor")])
    OP("vector", lambda e: e.drain())
    OP("sync", lambda e: e.dma_start(out=mods.rearrange("(a b) -> a b", a=48),
                                     in_=modr2[:, :]), dma="DS",
       deps=[DEP("vector")])
    mods_out = DMA("DS")
    OP("sync", lambda e: e.dma_start(
        out=ca_col[:, :], in_=mods[D:2 * D].rearrange("(f p) -> p f", p=128),
        allow_slow_non_contiguous=True), dma="DL", deps=[mods_out])
    OP("sync", lambda e: e.dma_start(
        out=sa_col[:, :], in_=mods[0:D].rearrange("(f p) -> p f", p=128),
        allow_slow_non_contiguous=True), dma="DL")
    cols_in = DMA("DL")
    OP("vector", lambda e: e.tensor_scalar(capca_c[:, :], ca_col[:, :], 1.0, None,
                                           ALU.add), deps=[cols_in])
    OP("vector", lambda e: e.drain())
    OP("vector", lambda e: e.tensor_copy(uvec_c[:, :], sa_col[:, :]))
    OP("vector", lambda e: e.drain())
    caps_done = DEP("vector")

    def gate_tile_transform(tile, scr):
        OP("vector", lambda e: e.tensor_scalar(scr, tile, -1.0, None, ALU.mult))
        OP("vector", lambda e: e.tensor_tensor(scr, tile, scr, ALU.max))
        OP("vector", lambda e: e.tensor_scalar(scr, scr, 0.25, None, ALU.max))
        OP("vector", lambda e: e.tensor_single_scalar(tile, tile, 0.0, ALU.is_ge))
        OP("vector", lambda e: e.tensor_scalar(tile, tile, 2.0, -1.0, ALU.mult,
                                               ALU.add))
        OP("vector", lambda e: e.tensor_tensor(tile, tile, scr, ALU.mult))
        OP("vector", lambda e: e.drain())

    OP("sync", lambda e: e.dma_start(
        out=gbrow[:, :], in_=mods[5 * D:6 * D].rearrange("(a b) -> a b", a=1)),
        dma="DL", deps=[mods_out])
    grow_in = DMA("DL")
    for h in range(2):
        p = pb0 if h == 0 else pb1
        t_write(p, lambda e, h=h, p=p: e.matmul(
            p[:, :], ones_r[:, :], gbrow[:, h * 512:(h + 1) * 512],
            start=True, stop=True), deps=[grow_in])
        consume(p, "vector", lambda e, h=h, p=p: e.tensor_copy(
            gb_t[:, h * 512:(h + 1) * 512], p[:, :]), deps=[DEP("tensor")])
    gate_tile_transform(gb_t[:, :], wmst[0][:, 0:D])

    smod.close()

    OP("sync", lambda e: e.dma_start(
        out=bst_c[:, :], in_=bst_in.rearrange("(f p) -> p f", p=128), allow_slow_non_contiguous=True), dma="DL")
    OP("sync", lambda e: e.dma_start(
        out=bgt_c[:, :], in_=bgt_in.rearrange("(f p) -> p f", p=128), allow_slow_non_contiguous=True), dma="DL")
    bld = DMA("DL")

    for wt_t, dstb in ((wts, bst_c), (wtg, bgt_c)):
        for k in range(8):
            p = pb0 if k % 2 == 0 else pb1
            for j in range(8):
                lw = (wt_t[:, j * D + k * 128:j * D + (k + 1) * 128], False)
                fn = (lambda e, j=j, k=k, p=p, wt_t=wt_t: e.matmul(
                    p[:, 0:1], wt_t[:, j * D + k * 128:j * D + (k + 1) * 128],
                    uvec_c[:, j:j + 1], start=(j == 0), stop=(j == 7)))
                if j == 0:
                    t_write(p, fn, [wt_done, caps_done], ldw=lw)
                else:
                    t_acc(fn, ldw=lw)
            consume(p, "vector", lambda e, k=k, p=p, dstb=dstb: e.tensor_tensor(
                dstb[:, k:k + 1], dstb[:, k:k + 1], p[:, 0:1], ALU.add),
                deps=[DEP("tensor"), bld])

    def ln_stats_norm(src_ap, wdep, part, dst_ap, bufs, dst_scalar_dep=None):
        negm_, ssq_, var_, ist_, nms_, t1_, sqs_ = bufs
        dd = [wdep] if wdep is not None else []
        rr = OP("vector", lambda e: e.reduce_sum(negm_, src_ap, axis=AX.X,
                                                 negate=True),
                deps=dd + [DEP("gpsimd")])
        OP("scalar", lambda e: e.activation(sqs_, src_ap, AF.Square,
                                            accum_out=ssq_),
           deps=dd + ([dst_scalar_dep] if dst_scalar_dep else []))
        sdep = DEP("scalar")
        OP("gpsimd", lambda e: e.tensor_scalar(negm_, negm_, 1.0 / D, None,
                                               ALU.mult), deps=[DOP(rr)])
        OP("gpsimd", lambda e: e.tensor_tensor(var_, negm_, negm_, ALU.mult))
        OP("gpsimd", lambda e: e.tensor_scalar(t1_, ssq_, 1.0 / D, None,
                                               ALU.mult), deps=[sdep])
        OP("gpsimd", lambda e: e.tensor_tensor(var_, t1_, var_, ALU.subtract))
        OP("scalar", lambda e, part=part: e.activation(
            t1_, var_, AF.Sqrt, bias=eps_c[0:part, 0:1]), deps=[DEP("gpsimd")])
        OP("vector", lambda e: e.reciprocal(ist_, t1_), deps=[DEP("scalar")])
        OP("gpsimd", lambda e: e.tensor_tensor(nms_, negm_, ist_, ALU.mult),
           deps=[DEP("vector")])
        gdep = DEP("gpsimd")
        OP("scalar", lambda e: e.activation(dst_ap, src_ap, AF.Identity,
                                            bias=nms_, scale=ist_),
           deps=[gdep])
        return DEP("scalar")

    sa = contextlib.ExitStack()
    xstF4 = sb("xstF4", [128, 4 * D], F32, sa)
    xF = xstF4.rearrange("p (s c) -> p s c", c=D)
    xnb = [sb("xnb0", [128, D], BF, sa), sb("xnb1", [128, D], BF, sa)]
    xnf2 = [sb("xnfa", [128, 8 * 512], BF, sa), sb("xnfb", [128, 8 * 512], BF, sa)]
    stats = [[sb(f"st{n}{p_}", [128, 4], F32, sa) for n in range(5)]
             for p_ in range(2)]

    tiles_a = [(ti * 512, 512) for ti in range(8)] + [(4096, 256)]
    xnf_reader = [None, None]
    xnb_reader = [None, None]
    xf_readers = [None] * 4
    ln_prev = [None, None]
    for ti, (c0, width) in enumerate(tiles_a):
        par = ti % 2
        nsub = width // 128
        xnf = xnf2[par]
        xnfr = xnf.rearrange("p (j c) -> p j c", j=8)
        negm4, ssq4, var4, ist4, nm4 = stats[par]
        wdeps = []
        for u in range(nsub):
            OP("sync", lambda e, c0=c0, u=u: e.dma_start(
                out=xF[:, u, :], in_=xq[c0 + u * 128:c0 + (u + 1) * 128, :]),
                dma="DL", deps=[xf_readers[u]])
            wdeps.append(DMA("DL"))
        for u in range(nsub):
            wdep = wdeps[u]
            rd_i = OP("vector", lambda e, u=u, negm4=negm4: e.reduce_sum(
                negm4[:, u:u + 1], xF[:, u, :], axis=AX.X, negate=True),
                deps=[wdep] + ([ln_prev[par]] if u == 0 else []))
            OP("scalar", lambda e, u=u, negm4=negm4, ssq4=ssq4: e.activation(
                xnb[(u + 1) % 2][:, :], xF[:, u, :], AF.Square,
                accum_out=ssq4[:, u:u + 1]),
                deps=[wdep, xnb_reader[(u + 1) % 2]]
                     + ([ln_prev[par]] if u == 0 else []))
        vred = DEP("vector")
        sdep = DEP("scalar")
        OP("gpsimd", lambda e, negm4=negm4, nsub=nsub: e.tensor_scalar(
            negm4[:, 0:nsub], negm4[:, 0:nsub], 1.0 / D, None, ALU.mult),
            deps=[vred])
        OP("gpsimd", lambda e, negm4=negm4, var4=var4, nsub=nsub: e.tensor_tensor(
            var4[:, 0:nsub], negm4[:, 0:nsub], negm4[:, 0:nsub], ALU.mult))
        OP("gpsimd", lambda e, ssq4=ssq4, nm4=nm4, nsub=nsub: e.tensor_scalar(
            nm4[:, 0:nsub], ssq4[:, 0:nsub], 1.0 / D, None, ALU.mult),
            deps=[sdep])
        OP("gpsimd", lambda e, nm4=nm4, var4=var4, nsub=nsub: e.tensor_tensor(
            var4[:, 0:nsub], nm4[:, 0:nsub], var4[:, 0:nsub], ALU.subtract))
        OP("scalar", lambda e, var4=var4, nm4=nm4, nsub=nsub: e.activation(
            nm4[:, 0:nsub], var4[:, 0:nsub], AF.Sqrt, bias=eps_c[:, 0:1]),
            deps=[DEP("gpsimd")])
        OP("vector", lambda e, nm4=nm4, ist4=ist4, nsub=nsub: e.reciprocal(
            ist4[:, 0:nsub], nm4[:, 0:nsub]), deps=[DEP("scalar")])
        OP("gpsimd", lambda e, negm4=negm4, ist4=ist4, nm4=nm4, nsub=nsub:
            e.tensor_tensor(nm4[:, 0:nsub], negm4[:, 0:nsub], ist4[:, 0:nsub],
                            ALU.mult), deps=[DEP("vector")])
        vdep = DEP("gpsimd")
        last_f = None
        for u in range(nsub):
            xb = xnb[u % 2]
            nrm = OP("scalar", lambda e, u=u, xb=xb, ist4=ist4, nm4=nm4:
                     e.activation(xb[:, :], xF[:, u, :], AF.Identity,
                                  bias=nm4[:, u:u + 1], scale=ist4[:, u:u + 1]),
                     deps=[vdep, xnb_reader[u % 2]])
            last_f = nrm
            xf_readers[u] = DOP(nrm)
            last_tp = None
            for ph in range(2):
                p = ptb0 if ph == 0 else ptb1
                for jj in range(4):
                    j = ph * 4 + jj
                    lw = (xb[:, j * 128:(j + 1) * 128], True)
                    fn = (lambda e, j=j, jj=jj, p=p, xb=xb: e.transpose(
                        p[:, jj * 128:(jj + 1) * 128],
                        xb[:, j * 128:(j + 1) * 128], identb[:, :]))
                    if jj == 0:
                        last_tp = t_write(p, fn, [DOP(nrm)], ldw=lw)
                    else:
                        last_tp = t_acc(fn, ldw=lw)
                for jj in range(4):
                    j = ph * 4 + jj
                    fn = (lambda e, j=j, jj=jj, u=u, p=p, xnfr=xnfr:
                          e.tensor_scalar(
                              xnfr[:, j, u * 128:(u + 1) * 128],
                              p[:, jj * 128:(jj + 1) * 128],
                              capca_c[:, j:j + 1], None, ALU.mult))
                    if jj < 3:
                        OP("vector", fn,
                           deps=([DEP("tensor"), xnf_reader[par], caps_done]
                                 if jj == 0 else []))
                    else:
                        consume(p, "vector", fn)
            xnb_reader[u % 2] = DOP(last_tp)
        ln_prev[par] = DOP(last_f)
        xnf_done = DEP("vector")
        last_mm = None
        for wt_t, b_c, is_g in ((wts, bst_c, False), (wtg, bgt_c, True)):
            for k in range(8):
                p = pm0 if k % 2 == 0 else pm1
                for j in range(8):
                    lw = (wt_t[:, j * D + k * 128:j * D + (k + 1) * 128], False)
                    fn = (lambda e, j=j, k=k, p=p, wt_t=wt_t, width=width,
                          xnf=xnf: e.matmul(
                              p[:, 0:width],
                              wt_t[:, j * D + k * 128:j * D + (k + 1) * 128],
                              xnf[:, j * 512:j * 512 + width],
                              start=(j == 0), stop=(j == 7)))
                    if j == 0:
                        last_mm = t_write(p, fn, [xnf_done], ldw=lw)
                    else:
                        last_mm = t_acc(fn, ldw=lw)
                tdep = DEP("tensor")
                if not is_g:
                    st0, nst = c0 // SI, width // SI
                    consume(p, "scalar", lambda e, k=k, p=p, st0=st0, nst=nst,
                            width=width: e.activation(
                                pxi[:, :, k, st0:st0 + nst].rearrange(
                                    "p i s -> p s i"),
                                p[:, 0:width],
                                AF.Identity, bias=bst_c[:, k:k + 1]),
                            deps=[tdep])
                else:
                    lo, hi = max(c0, WM - 1), min(c0 + width, WM - 1 + GW)
                    if lo < hi:
                        consume(p, "scalar",
                                lambda e, k=k, p=p, c0=c0, lo=lo, hi=hi:
                                e.activation(gr[:, k, lo - (WM - 1):hi - (WM - 1)],
                                             p[:, lo - c0:hi - c0], AF.Sigmoid,
                                             bias=bgt_c[:, k:k + 1]),
                                deps=[tdep])
        xnf_reader[par] = DOP(last_mm)
    sa.close()
    s1w.close()
    a_done = DEP("scalar")

    smean = contextlib.ExitStack()
    mst = [sb(f"mstv{v}", [128, D], F32, smean, side="right")
           for v in range(2)]

    def meanabs(tiles, total, inv_dst, scale_dst):
        OP("gpsimd", lambda e: e.memset(acc_c[:, :], 0.0), deps=[a_done])
        readers = [None, None]
        for i, src_ in enumerate(tiles):
            stg = mst[i % 2]
            OP("sync", lambda e, src_=src_, stg=stg: e.dma_start(
                out=stg[:, :],
                in_=src_), dma="DL",
                deps=[readers[i % 2], a_done])
            wdep = DMA("DL")
            rr = OP("vector", lambda e, stg=stg: e.reduce_sum(
                red_c[:, :], stg[:, :], axis=AX.X, apply_absolute_value=True),
                deps=[wdep, DEP("gpsimd")])
            r = OP("gpsimd", lambda e: e.tensor_tensor(acc_c[:, :], acc_c[:, :],
                                                       red_c[:, :], ALU.add),
                   deps=[DOP(rr)])
            readers[i % 2] = DOP(r)
        gdep = DEP("gpsimd")
        t_write(pb0, lambda e: e.matmul(pb0[0:1, 0:1], acc_c[:, :], one_c[:, :],
                                        start=True, stop=True), deps=[gdep])
        consume(pb0, "vector", lambda e: e.tensor_scalar(
            sw_r[:, :], pb0[0:1, 0:1], 1.0 / total, None, ALU.mult),
            deps=[DEP("tensor")])
        OP("gpsimd", lambda e: e.tensor_scalar(sw_r[:, :], sw_r[:, :], EPS, None,
                                               ALU.max), deps=[DEP("vector")])
        gdep = DEP("gpsimd")
        t_write(pb1, lambda e: e.matmul(pb1[:, 0:1], ones_r[:, :], sw_r[:, :],
                                        start=True, stop=True), deps=[gdep])
        consume(pb1, "vector", lambda e: e.tensor_copy(scale_dst[:, :],
                                                       pb1[:, 0:1]),
                deps=[DEP("tensor")])
        OP("vector", lambda e: e.drain())
        OP("vector", lambda e: e.reciprocal(inv_dst[:, :], scale_dst[:, :]))
        OP("vector", lambda e: e.drain())

    w2v = w2.rearrange("a (n c) -> (a n) c", c=D)
    meanabs([w1[i * 128:(i + 1) * 128, :] for i in range(32)], DH * D,
            sq1_c, swc1)
    meanabs([w2v[i * 128:(i + 1) * 128, :] for i in range(32)], D * DH,
            sq2_c, swc2)
    smean.close()

    ss = contextlib.ExitStack()
    ws = sb("ws", [128, 8 * SEG], F16, ss)
    tmp_h = [sb("tmph0", [128, 4 * SEG], F16, ss),
             sb("tmph1", [128, 4 * SEG], F16, ss)]
    etmp = sb("etmp", [128, 8], F32, ss)
    segmask = sb("segmask", [128, 8 * SEG], F16, ss)
    wsr = ws.rearrange("p (j s) -> p j s", j=8)
    OP("sync", lambda e: e.dma_start(out=segmask[:, :], in_=segmask_in),
       dma="DL", deps=[a_done])
    segm_ld = DMA("DL")

    halves = ((0, 4), (4, 8))
    sprev = [None, None]
    for w in range(WM):
        iw, dw = w % SI, w // SI
        for hi, (j0, j1) in enumerate(halves):
            tmp = tmp_h[hi]
            pslice = pxi[:, iw, j0:j1, dw:dw + SEG]
            wslice = wsr[:, j0:j1, :]
            if w == 0:
                v = OP("vector", lambda e, tmp=tmp, pslice=pslice: e.tensor_copy(
                    tmp[:, :], pslice), deps=[a_done])
            else:
                v = OP("vector", lambda e, tmp=tmp, pslice=pslice, wslice=wslice:
                       e.tensor_tensor(tmp[:, :], wslice, pslice, ALU.add),
                       deps=[sprev[hi]])
            s = OP("scalar", lambda e, tmp=tmp, wslice=wslice: e.activation(
                wslice, tmp[:, :], AF.Tanh), deps=[DOP(v)])
            sprev[hi] = DOP(s)
    OP("vector", lambda e: e.tensor_copy(sedge[:, :], ws[:, 0:8 * SEG:SEG]),
       deps=[sprev[0], sprev[1]])
    OP("vector", lambda e: e.drain())
    OP("vector", lambda e: e.tensor_tensor(ws[:, :], ws[:, :], segmask[:, :],
                                           ALU.mult), deps=[segm_ld])
    OP("vector", lambda e: e.drain())
    mask_done = DEP("vector")
    sprev = [mask_done, mask_done]
    for jj in range(SEG):
        for hi, (j0, j1) in enumerate(halves):
            tmp = tmp_h[hi]
            cur = pxi[:, jj, j0:j1, 3:3 + SEG]
            if jj == 0:
                prev = wsr[:, j0:j1, :]
            else:
                prev = pxi[:, jj - 1, j0:j1, 3:3 + SEG]
            v = OP("vector", lambda e, tmp=tmp, prev=prev, cur=cur:
                   e.tensor_tensor(tmp[:, :], prev, cur, ALU.add),
                   deps=[sprev[hi]])
            s = OP("scalar", lambda e, tmp=tmp, cur=cur: e.activation(
                cur, tmp[:, :], AF.Tanh), deps=[DOP(v)])
            sprev[hi] = DOP(s)
    OP("vector", lambda e: e.tensor_tensor(
        etmp[:, :], pxi[:, SI - 1, :, SJ - 2:SJ - 1], pxi[:, 0, :, SJ - 1:SJ],
        ALU.add), deps=[sprev[0], sprev[1]])
    OP("scalar", lambda e: e.activation(pxi[:, 0, :, SJ - 1:SJ], etmp[:, :],
                                        AF.Tanh), deps=[DEP("vector")])
    scan_done = DEP("scalar")
    ss.close()

    if debug:
        OP("sync", lambda e: e.dma_start(out=dbg_px, in_=px[:, :]), dma="DS",
           deps=[scan_done])
        OP("sync", lambda e: e.dma_start(out=dbg_g, in_=g[:, :]), dma="DS",
           deps=[a_done])

    sc2 = contextlib.ExitStack()
    attnb = [sb("attnb0", [128, 128], BF, sc2), sb("attnb1", [128, 128], BF, sc2)]
    xst2 = [sb("xst2a", [128, D], F32, sc2), sb("xst2b", [128, D], F32, sc2)]
    x2t = [sb("x2ta", [128, D], F32, sc2), sb("x2tb", [128, D], F32, sc2)]
    h2t = sb("h2t", [128, D], F32, sc2)
    ki32 = sb("ki32", [128, D], I32, sc2)
    kxtm = [sb("kxtma", [128, D], BF, sc2), sb("kxtmb", [128, D], BF, sc2)]
    kxst = [sb("kxsta", [128, 8 * 128], BF, sc2),
            sb("kxstb", [128, 8 * 128], BF, sc2)]
    amax_c = sb("amax_c", [128, 1], F32, sc2)
    sct_c = sb("sct_c", [128, 1], F32, sc2)
    isct_c = sb("isct_c", [128, 1], F32, sc2)
    A2 = sb("A2", [128, D], F32, sc2)
    B2 = sb("B2", [128, D], F32, sc2)
    ga_t = sb("ga_t", [128, D], F32, sc2)
    ga_row = sb("ga_row", [1, D], F32, sc2)
    cb_row = sb("cb_row", [1, D], F32, sc2)
    sb_row = sb("sb_row", [1, D], F32, sc2)
    sgE = sb("sgE", [128, 16], F32, sc2)
    keg = sb("keg", [128, 16], BF, sc2)
    kegr = keg.rearrange("p (j c) -> p j c", j=8)
    negm2 = sb("negm2", [128, 1], F32, sc2)
    ssq2 = sb("ssq2", [128, 1], F32, sc2)
    var2 = sb("var2", [128, 1], F32, sc2)
    ist2 = sb("ist2", [128, 1], F32, sc2)
    nm2s = sb("nm2s", [128, 1], F32, sc2)
    t2c = sb("t2c", [128, 1], F32, sc2)
    sq2scr = sb("sq2scr", [128, D], F32, sc2)
    h2p = [h2t, sb("h2tb", [128, D], F32, sc2)]
    ki2p = [ki32, sb("ki32b", [128, D], I32, sc2)]
    amaxp = [amax_c, sb("amaxb", [128, 1], F32, sc2)]
    sctp = [sct_c, sb("sctb", [128, 1], F32, sc2)]
    isctp = [isct_c, sb("isctb", [128, 1], F32, sc2)]
    negmp = [negm2, sb("negm2b", [128, 1], F32, sc2)]
    ssqp = [ssq2, sb("ssq2b", [128, 1], F32, sc2)]
    varp = [var2, sb("var2b", [128, 1], F32, sc2)]
    istp = [ist2, sb("ist2b", [128, 1], F32, sc2)]
    nmsp = [nm2s, sb("nm2sb", [128, 1], F32, sc2)]
    t2p = [t2c, sb("t2cb", [128, 1], F32, sc2)]

    OP("sync", lambda e: e.dma_start(
        out=ga_row[:, :], in_=mods[2 * D:3 * D].rearrange("(a b) -> a b", a=1)),
        dma="DL", deps=[mods_out, scan_done])
    OP("sync", lambda e: e.dma_start(
        out=cb_row[:, :], in_=mods[4 * D:5 * D].rearrange("(a b) -> a b", a=1)),
        dma="DL")
    OP("sync", lambda e: e.dma_start(
        out=sb_row[:, :], in_=mods[3 * D:4 * D].rearrange("(a b) -> a b", a=1)),
        dma="DL")
    rows_in = DMA("DL")
    for base, dst in ((cb_row, A2), (sb_row, B2), (ga_row, ga_t)):
        for h in range(2):
            p = pb0 if h == 0 else pb1
            rhs = base[:, h * 512:(h + 1) * 512]
            t_write(p, lambda e, rhs=rhs, p=p: e.matmul(
                p[:, :], ones_r[:, :], rhs, start=True, stop=True),
                deps=[rows_in])
            consume(p, "vector", lambda e, dst=dst, h=h, p=p: e.tensor_copy(
                dst[:, h * 512:(h + 1) * 512], p[:, :]), deps=[DEP("tensor")])
    OP("vector", lambda e: e.tensor_scalar(A2[:, :], A2[:, :], 1.0, None, ALU.add))
    gate_tile_transform(ga_t[:, :], h2t[:, :])
    bcast_done = DEP("vector")

    ln2_bufs = (negm2[:, :], ssq2[:, :], var2[:, :], ist2[:, :], nm2s[:, :],
                t2c[:, :], sq2scr[:, :])
    ln2e_bufs = tuple(b[0:2, :] for b in ln2_bufs)

    def quant_block(x2_ap, kx_dst, part, bufs, sq_dep, kx_dep=None):
        nd = ln_stats_norm(x2_ap, DEP("vector"), part, h2t[0:part, :], bufs,
                           dst_scalar_dep=sq_dep)
        OP("vector", lambda e, part=part: e.tensor_tensor(
            h2t[0:part, :], h2t[0:part, :], A2[0:part, :], ALU.mult),
            deps=[nd, bcast_done])
        OP("vector", lambda e, part=part: e.tensor_tensor(
            h2t[0:part, :], h2t[0:part, :], B2[0:part, :], ALU.add))
        rr = OP("vector", lambda e, part=part: e.reduce_max(
            amax_c[0:part, :], h2t[0:part, :], axis=AX.X,
            apply_absolute_value=True), deps=[DEP("gpsimd")])
        OP("gpsimd", lambda e, part=part: e.tensor_scalar(
            amax_c[0:part, :], amax_c[0:part, :], EPS, None, ALU.max),
            deps=[DOP(rr)])
        OP("gpsimd", lambda e, part=part: e.tensor_scalar(
            isct_c[0:part, :], amax_c[0:part, :], 1.0 / 127.0, None, ALU.mult))
        OP("vector", lambda e, part=part: e.reciprocal(
            sct_c[0:part, :], amax_c[0:part, :]), deps=[DEP("gpsimd")])
        OP("gpsimd", lambda e, part=part: e.tensor_scalar(
            sct_c[0:part, :], sct_c[0:part, :], 127.0, None, ALU.mult),
            deps=[DEP("vector")])
        gdep = DEP("gpsimd")
        OP("vector", lambda e, part=part: e.tensor_scalar(
            ki32[0:part, :], h2t[0:part, :], sct_c[0:part, 0:1], None, ALU.mult),
            deps=[gdep])
        OP("vector", lambda e, part=part, kx_dst=kx_dst: e.tensor_scalar(
            kx_dst, ki32[0:part, :], isct_c[0:part, 0:1], None, ALU.mult),
            deps=[kx_dep])
        return DEP("vector")

    x2dma = [scan_done, scan_done]
    kxdma = [scan_done, scan_done]
    tp_attn = [None, None]
    gG = [dict(), dict()]
    x2done = [None, None]

    def emit_attn_A_products(sub):
        t0 = sub * 128
        par = sub % 2
        st0 = t0 // SI + 3
        for j in range(8):
            ab = attnb[j % 2]
            gap = gr[:, j, t0 + 1:t0 + 129]
            pap = pxi[:, :, j, st0:st0 + 2].rearrange("p i s -> p s i")
            v = OP("vector", lambda e, ab=ab, gap=gap, pap=pap: e.tensor_tensor(
                ab[:, :], gap, pap, ALU.mult),
                deps=[scan_done, tp_attn[j % 2]])
            p = ptb0 if j < 4 else ptb1
            lw = (ab[:, :], True)
            fn = (lambda e, j=j, p=p, ab=ab: e.transpose(
                p[:, (j % 4) * 128:(j % 4 + 1) * 128], ab[:, :], identb[:, :]))
            if j % 4 == 0:
                i = t_write(p, fn, [DOP(v)], ldw=lw)
            else:
                i = OP("tensor", fn, [DOP(v)], ldw=lw)
            tp_attn[j % 2] = DOP(i)
        OP("sync", lambda e, t0=t0, par=par: e.dma_start(
            out=xst2[par][:, :], in_=xq[WM + t0:WM + t0 + 128, :]), dma="DL",
            deps=[x2dma[par]])
        return DMA("DL")

    def emit_attn_A_rest(sub, xdep):
        t0 = sub * 128
        par = sub % 2
        tdep = DEP("tensor")
        consume(ptb0, "vector", lambda e, par=par: e.tensor_tensor(
            x2t[par][:, 0:512], ptb0[:, 0:512], ga_t[:, 0:512], ALU.mult),
            deps=[tdep, bcast_done, x2dma[par], gG[par].get("norm")])
        consume(ptb1, "vector", lambda e, par=par: e.tensor_tensor(
            x2t[par][:, 512:1024], ptb1[:, 0:512], ga_t[:, 512:1024], ALU.mult))
        a = OP("vector", lambda e, par=par: e.tensor_tensor(
            x2t[par][:, :], x2t[par][:, :], xst2[par][:, :], ALU.add),
            deps=[xdep])
        i = OP("scalar", lambda e, t0=t0, par=par: e.dma_start(
            out=x2s[t0:t0 + 128, :], in_=x2t[par][:, :]), dma="DS",
            deps=[DEP("vector")])
        x2dma[par] = sc.ops[i].dma_tok
        x2done[par] = DOP(a)

    def emit_attn_B_early(sub):
        par = sub % 2
        g_ = gG[par]
        rr = OP("vector", lambda e, par=par: e.reduce_sum(
            negmp[par][:, :], x2t[par][:, :], axis=AX.X, negate=True),
            deps=[x2done[par], g_.get("nms")])
        sq = OP("scalar", lambda e, par=par: e.activation(
            sq2scr[:, :], x2t[par][:, :], AF.Square, accum_out=ssqp[par][:, :]),
            deps=[x2done[par], g_.get("t1r")])
        OP("gpsimd", lambda e, par=par: e.tensor_scalar(
            negmp[par][:, :], negmp[par][:, :], 1.0 / D, None, ALU.mult),
            deps=[DOP(rr)])
        OP("gpsimd", lambda e, par=par: e.tensor_tensor(
            varp[par][:, :], negmp[par][:, :], negmp[par][:, :], ALU.mult))
        t1g = OP("gpsimd", lambda e, par=par: e.tensor_scalar(
            t2p[par][:, :], ssqp[par][:, :], 1.0 / D, None, ALU.mult),
            deps=[DOP(sq), g_.get("recip")])
        OP("gpsimd", lambda e, par=par: e.tensor_tensor(
            varp[par][:, :], t2p[par][:, :], varp[par][:, :], ALU.subtract))
        sqr = OP("scalar", lambda e, par=par: e.activation(
            t2p[par][:, :], varp[par][:, :], AF.Sqrt, bias=eps_c[:, 0:1]),
            deps=[DEP("gpsimd")])
        g_["t1r"] = DOP(t1g)
        g_["sqrt"] = DOP(sqr)

    def emit_attn_B_mid(sub):
        par = sub % 2
        g_ = gG[par]
        rc = OP("vector", lambda e, par=par: e.reciprocal(
            istp[par][:, :], t2p[par][:, :]),
            deps=[g_["sqrt"], g_.get("norm")])
        g_["recip"] = DOP(rc)
        nm = OP("gpsimd", lambda e, par=par: e.tensor_tensor(
            nmsp[par][:, :], negmp[par][:, :], istp[par][:, :], ALU.mult),
            deps=[DOP(rc)])
        g_["nms"] = DOP(nm)
        nrm = OP("scalar", lambda e, par=par: e.activation(
            h2p[par][:, :], x2t[par][:, :], AF.Identity,
            bias=nmsp[par][:, :], scale=istp[par][:, :]),
            deps=[DOP(nm), g_.get("rmax")])
        g_["norm"] = DOP(nrm)
        OP("gpsimd", lambda e, par=par: e.tensor_tensor(
            h2p[par][:, 512:1024], h2p[par][:, 512:1024], A2[:, 512:1024],
            ALU.mult), deps=[DOP(nrm), bcast_done])
        ga2 = OP("gpsimd", lambda e, par=par: e.tensor_tensor(
            h2p[par][:, 512:1024], h2p[par][:, 512:1024], B2[:, 512:1024],
            ALU.add))
        g_["gah1"] = DOP(ga2)

    def emit_attn_B_late2(sub):
        par = sub % 2
        g_ = gG[par]
        OP("vector", lambda e, par=par: e.tensor_tensor(
            h2p[par][:, 0:512], h2p[par][:, 0:512], A2[:, 0:512], ALU.mult),
            deps=[g_["norm"], bcast_done])
        OP("vector", lambda e, par=par: e.tensor_tensor(
            h2p[par][:, 0:512], h2p[par][:, 0:512], B2[:, 0:512], ALU.add))
        rm = OP("vector", lambda e, par=par: e.reduce_max(
            amaxp[par][:, :], h2p[par][:, :], axis=AX.X,
            apply_absolute_value=True), deps=[g_["gah1"], g_.get("isct")])
        g_["rmax"] = DOP(rm)
        OP("gpsimd", lambda e, par=par: e.tensor_scalar(
            amaxp[par][:, :], amaxp[par][:, :], EPS, None, ALU.max),
            deps=[DOP(rm)])
        ic = OP("gpsimd", lambda e, par=par: e.tensor_scalar(
            isctp[par][:, :], amaxp[par][:, :], 1.0 / 127.0, None, ALU.mult),
            deps=[g_.get("kxv")])
        g_["isct"] = DOP(ic)
        rs = OP("vector", lambda e, par=par: e.reciprocal(
            sctp[par][:, :], amaxp[par][:, :]),
            deps=[DEP("gpsimd"), g_.get("ki")])
        sc7 = OP("gpsimd", lambda e, par=par: e.tensor_scalar(
            sctp[par][:, :], sctp[par][:, :], 127.0, None, ALU.mult),
            deps=[DOP(rs)])
        ki_i = OP("scalar", lambda e, par=par: e.activation(
            ki2p[par][:, :], h2p[par][:, :], AF.Identity,
            scale=sctp[par][:, 0:1]), deps=[DOP(sc7), g_.get("kxv")])
        g_["ki"] = DOP(ki_i)
        kxv = OP("vector", lambda e, par=par: e.tensor_scalar(
            kxtm[par][:, :], ki2p[par][:, :], isctp[par][:, 0:1], None,
            ALU.mult), deps=[DOP(ki_i), g_.get("kxtp")])
        g_["kxv"] = DOP(kxv)
        qdep = DOP(kxv)
        last_tp = None
        for ph in range(2):
            p = ptb0 if ph == 0 else ptb1
            for jj in range(4):
                j = ph * 4 + jj
                lw = (kxtm[par][:, j * 128:(j + 1) * 128], True)
                fn = (lambda e, j=j, jj=jj, p=p, par=par: e.transpose(
                    p[:, jj * 128:(jj + 1) * 128],
                    kxtm[par][:, j * 128:(j + 1) * 128], identb[:, :]))
                if jj == 0:
                    i = t_write(p, fn, [qdep], ldw=lw)
                else:
                    i = t_acc(fn, ldw=lw)
            last_tp = i
            consume(p, "scalar", lambda e, ph=ph, p=p, par=par: e.activation(
                kxst[par].rearrange("p (j c) -> p j c", j=8)[:, ph * 4:(ph + 1) * 4, :],
                p[:, 0:512], AF.Identity),
                deps=[DEP("tensor"), kxdma[par]])
        g_["kxtp"] = DOP(last_tp)
        t0 = sub * 128
        i = OP("scalar", lambda e, t0=t0, par=par: e.dma_start(
            out=kxsr[:, :, t0 + 1:t0 + 129],
            in_=kxst[par].rearrange("p (j c) -> p j c", j=8)), dma="DS",
            deps=[DEP("scalar")])
        kxdma[par] = sc.ops[i].dma_tok

    for sub in range(32):
        if sub >= 1:
            emit_attn_B_early(sub - 1)
        xdep = emit_attn_A_products(sub)
        if sub >= 1:
            emit_attn_B_mid(sub - 1)
        emit_attn_A_rest(sub, xdep)
        if sub >= 1:
            emit_attn_B_late2(sub - 1)
    emit_attn_B_early(31)
    emit_attn_B_mid(31)
    emit_attn_B_late2(31)

    for j in range(8):
        OP("vector", lambda e, j=j: e.tensor_tensor(
            sgE[:, 2 * j:2 * j + 1], gr[:, j, 0:1], sedge[:, j:j + 1], ALU.mult),
            deps=[scan_done])
        OP("vector", lambda e, j=j: e.tensor_tensor(
            sgE[:, 2 * j + 1:2 * j + 2], gr[:, j, GW - 1:GW],
            pxi[:, 0, j, SJ - 1:SJ], ALU.mult))
    OP("vector", lambda e: e.drain())
    sg_dep = DEP("vector")
    OP("sync", lambda e: e.dma_start(
        out=xst2[0][0:2, :], in_=xq[WM - 1:WM + T + 1:T + 1, :]), dma="DL",
        deps=[x2dma[0]])
    xe_dep = DMA("DL")
    for j in range(8):
        p = pt0 if j < 4 else pt1
        fn = (lambda e, j=j, p=p: e.transpose(
            p[0:2, (j % 4) * 128:(j % 4 + 1) * 128], sgE[:, 2 * j:2 * j + 2],
            ident32[:, :]))
        if j % 4 == 0:
            t_write(p, fn, [sg_dep])
        else:
            t_acc(fn)
    tdep = DEP("tensor")
    consume(pt0, "vector", lambda e: e.tensor_tensor(
        x2t[0][0:2, 0:512], pt0[0:2, :], ga_t[0:2, 0:512], ALU.mult),
        deps=[tdep, bcast_done, x2dma[0], gG[0].get("norm")])
    consume(pt1, "vector", lambda e: e.tensor_tensor(
        x2t[0][0:2, 512:1024], pt1[0:2, :], ga_t[0:2, 512:1024], ALU.mult))
    OP("vector", lambda e: e.tensor_tensor(x2t[0][0:2, :], x2t[0][0:2, :],
                                           xst2[0][0:2, :], ALU.add),
       deps=[xe_dep])
    quant_block(x2t[0][0:2, :], kxtm[0][0:2, :], 2, ln2e_bufs, DEP("scalar"),
                kx_dep=gG[0].get("kxtp"))
    OP("vector", lambda e: e.tensor_scalar(kxtm[0][0:2, :], kxtm[0][0:2, :],
                                           edgemask[0:2, 0:1], None, ALU.mult),
       deps=[ld0])
    qdep = DEP("vector")
    for j in range(8):
        p = ptb0 if j < 4 else ptb1
        lw = (kxtm[0][0:2, j * 128:(j + 1) * 128], True)
        fn = (lambda e, j=j, p=p: e.transpose(
            p[:, (j % 4) * 2:(j % 4) * 2 + 2], kxtm[0][0:2, j * 128:(j + 1) * 128],
            identb[0:2, 0:2]))
        if j % 4 == 0:
            t_write(p, fn, [qdep], ldw=lw)
        else:
            t_acc(fn, ldw=lw)
    tdep = DEP("tensor")
    consume(ptb0, "vector", lambda e: e.tensor_copy(kegr[:, 0:4, :], ptb0[:, 0:8]),
            deps=[tdep])
    consume(ptb1, "vector", lambda e: e.tensor_copy(kegr[:, 4:8, :],
                                                    ptb1[:, 0:8]))
    OP("sync", lambda e: e.dma_start(out=kxsr[:, :, 0:1], in_=kegr[:, :, 0:1],
                                     allow_slow_non_contiguous=True),
       dma="DS", deps=[DEP("vector")])
    OP("sync", lambda e: e.dma_start(out=kxsr[:, :, GW - 1:GW],
                                     in_=kegr[:, :, 1:2],
                                     allow_slow_non_contiguous=True), dma="DS")

    sc2.close()
    s1.close()
    attn_barrier = sc.barrier_deps()

    s2 = contextlib.ExitStack()
    kw1t = sb("kw1t", [128, 8 * DH], BF, s2)
    kw2t = sb("kw2t", [128, 32 * D], BF, s2)

    sp = contextlib.ExitStack()
    wq = [sb(f"wq{v}", [128, D], F32, sp) for v in range(4)]
    wiq = sb("wiq", [128, D], I32, sp)
    wqb = [sb(f"wqb{v}", [128, D], BF, sp) for v in range(4)]

    def kw_prep(srcs, sq_col, dst, chunk_base, deps0):
        n = len(srcs)
        for i4 in range(n // 4):
            for v in range(4):
                i = i4 * 4 + v
                OP("sync", lambda e, src=srcs[i], stg=wq[v]: e.dma_start(
                    out=stg[:, :], in_=src), dma="DL",
                    deps=list(deps0) + [DEP("tensor")])
                wdep = DMA("DL")
                OP("vector", lambda e, stg=wq[v]: e.tensor_scalar(
                    wiq[:, :], stg[:, :], sq_col[:, 0:1], None, ALU.mult),
                    deps=[wdep])
                OP("vector", lambda e: e.tensor_scalar(wiq[:, :], wiq[:, :], 1,
                                                       None, ALU.min))
                OP("vector", lambda e: e.tensor_scalar(wiq[:, :], wiq[:, :], -1,
                                                       None, ALU.max))
                OP("vector", lambda e, v=v: e.tensor_copy(wqb[v][:, :], wiq[:, :]))
            qdep = DEP("vector")
            for j in range(8):
                p = ptb0 if j % 2 == 0 else ptb1
                for v in range(4):
                    lw = (wqb[v][:, j * 128:(j + 1) * 128], True)
                    fn = (lambda e, j=j, v=v, p=p: e.transpose(
                        p[:, v * 128:(v + 1) * 128],
                        wqb[v][:, j * 128:(j + 1) * 128], identb[:, :]))
                    if v == 0:
                        t_write(p, fn, [qdep], ldw=lw)
                    else:
                        t_acc(fn, ldw=lw)
                cb_ = chunk_base(j)
                consume(p, "vector", lambda e, cb_=cb_, i4=i4, p=p: e.tensor_copy(
                    dst[:, cb_ + i4 * 512:cb_ + (i4 + 1) * 512], p[:, 0:512]),
                    deps=[DEP("tensor")])

    kw_prep([w1[i * 128:(i + 1) * 128, :] for i in range(32)], sq1_c, kw1t,
            lambda j: j * DH, attn_barrier)
    for qq in range(4):
        kw_prep([w2[i * 128:(i + 1) * 128, qq * D:(qq + 1) * D] for i in range(8)],
                sq2_c, kw2t, lambda j, qq=qq: (qq * 8 + j) * D, [])
    sp.close()

    y = [sb("y0", [128, 32 * 258], BF, s2), sb("y1", [128, 32 * 258], BF, s2)]
    yr = [t.rearrange("p (h c) -> p h c", h=32) for t in y]
    kht = sb("kht", [128, 32 * MT], BF, s2)
    khtr = kht.rearrange("p (h c) -> p h c", h=32)
    kxt = [sb("kxt0", [128, 8 * 258], BF, s2), sb("kxt1", [128, 8 * 258], BF, s2)]
    kxtr = [t.rearrange("p (j c) -> p j c", j=8) for t in kxt]
    x2rt = sb("x2rt", [128, D], F32, s2)
    NCONV = 4
    convo = [sb(f"convo{v}", [128, MT], BF, s2) for v in range(NCONV)]
    mmax = sb("mmax", [128, MT], BF, s2)
    mmin = sb("mmin", [128, MT], BF, s2)
    khi = sb("khi", [128, MT], I32, s2)
    mcol = sb("mcol", [128, 1], F32, s2)
    rcol = sb("rcol", [128, 1], F32, s2)
    s2c = sb("s2c", [128, 1], F32, s2)
    i2cc = sb("i2cc", [128, 2], F32, s2)
    rowbuf = sb("rowbuf", [1, MT], F32, s2)
    otb = sb("otb", [128, 8 * MT], BF, s2)
    outt = sb("outt", [128, D], F32, s2)

    kx_ready = sc.ddma("DS")
    conv_reader = [[None] * 32, [None] * 32]
    w1_mm_last = [None, None]
    kht_reader = [None]
    mm_reader = [None]
    silu_ops = [None] * 32
    silu_hist = [None] * NCONV

    def emit_w1_group(m, h, wdep):
        buf = m % 2
        p = pm0 if h % 2 == 0 else pm1
        last = None
        for j in range(8):
            lw = (kw1t[:, j * DH + h * 128:j * DH + (h + 1) * 128], False)
            fn = (lambda e, h=h, j=j, p=p, buf=buf: e.matmul(
                p[:, 0:258], kw1t[:, j * DH + h * 128:j * DH + (h + 1) * 128],
                kxtr[buf][:, j, :], start=(j == 0), stop=(j == 7)))
            if j == 0:
                last = t_write(p, fn, [wdep], ldw=lw)
            else:
                last = t_acc(fn, ldw=lw)
        consume(p, "scalar", lambda e, h=h, p=p, buf=buf: e.activation(
            yr[buf][:, h, :], p[:, 0:258], AF.Identity,
            scale=swc1[:, 0:1]), deps=[DEP("tensor"), conv_reader[buf][h]])
        return last, DEP("scalar")

    def emit_conv(mm, h, evac_dep):
        buf = mm % 2
        cv_ = convo[h % NCONV]
        OP("vector", lambda e, h=h, buf=buf, cv_=cv_: e.tensor_scalar(
            cv_[:, :], yr[buf][:, h, 0:256], cw_c[:, 3 * h:3 * h + 1],
            None, ALU.mult), deps=[evac_dep, silu_hist[h % NCONV]])
        OP("vector", lambda e, h=h, buf=buf, cv_=cv_: e.scalar_tensor_tensor(
            cv_[:, :], yr[buf][:, h, 1:257], cw_c[:, 3 * h + 1:3 * h + 2],
            cv_[:, :], ALU.mult, ALU.add))
        i = OP("vector", lambda e, h=h, buf=buf, cv_=cv_: e.scalar_tensor_tensor(
            cv_[:, :], yr[buf][:, h, 2:258], cw_c[:, 3 * h + 2:3 * h + 3],
            cv_[:, :], ALU.mult, ALU.add))
        conv_reader[buf][h] = DOP(i)
        si = OP("scalar", lambda e, h=h, cv_=cv_: e.activation(
            khtr[:, h, :], cv_[:, :], AF.Silu, bias=cb_c[:, h:h + 1]),
            deps=[DOP(i), kht_reader[0]])
        silu_hist[h % NCONV] = DOP(si)
        silu_ops[h] = DOP(si)

    def emit_maxmin(h):
        OP("vector", lambda e, h=h: e.tensor_tensor(
            mmax[:, :], mmax[:, :], khtr[:, h, :], ALU.max),
            deps=[silu_ops[h]])
        OP("vector", lambda e, h=h: e.tensor_tensor(
            mmin[:, :], mmin[:, :], khtr[:, h, :], ALU.min))

    def merged_tile(m, evac_prev):
        have_w1 = m < NMT
        have_conv = m >= 1
        evac_cur = [None] * 32
        wdep = None
        if have_w1:
            buf = m % 2
            OP("sync", lambda e, m=m, buf=buf: e.dma_start(
                out=kxtr[buf][:, :, :], in_=kxsr[:, :, m * MT:m * MT + 258]),
                dma="DL", deps=[kx_ready, w1_mm_last[buf]])
            wdep = DMA("DL")
        if have_conv:
            OP("vector", lambda e: e.memset(mmax[:, :], 0.0),
               deps=[mm_reader[0]])
            OP("vector", lambda e: e.memset(mmin[:, :], 0.0))
        last_mm = None
        for h in range(32):
            if have_w1:
                last_mm, ev = emit_w1_group(m, h, wdep)
                evac_cur[h] = ev
            if have_conv:
                emit_conv(m - 1, h, evac_prev[h])
                if h >= 1:
                    emit_maxmin(h - 1)
        if have_w1:
            w1_mm_last[m % 2] = DOP(last_mm)
        if have_conv:
            emit_maxmin(31)
        return evac_cur

    def mid_quant(m):
        OP("vector", lambda e: e.tensor_scalar(mmin[:, :], mmin[:, :], -1.0,
                                               None, ALU.mult))
        OP("vector", lambda e: e.tensor_tensor(mmax[:, :], mmax[:, :],
                                               mmin[:, :], ALU.max))
        OP("vector", lambda e: e.drain())
        first_tp = True
        for u in range(2):
            pT = ptb0 if u == 0 else ptb1
            mm_reader[0] = DOP(t_write(pT, lambda e, u=u, pT=pT: e.transpose(
                pT[:, 0:128], mmax[:, u * 128:(u + 1) * 128], identb[:, :]),
                deps=[DEP("vector")],
                ldw=(mmax[:, u * 128:(u + 1) * 128], True)))
            rr = consume(pT, "vector", lambda e, pT=pT: e.reduce_max(
                mcol[:, :], pT[:, 0:128], axis=AX.X),
                deps=[DEP("tensor"), DEP("gpsimd")])
            OP("gpsimd", lambda e: e.tensor_scalar(mcol[:, :], mcol[:, :], EPS,
                                                   None, ALU.max),
               deps=[DOP(rr)])
            OP("gpsimd", lambda e, u=u: e.tensor_scalar(
                i2cc[:, u:u + 1], mcol[:, :], 1.0 / 127.0, None, ALU.mult))
            OP("gpsimd", lambda e, u=u: e.tensor_tensor(
                i2cc[:, u:u + 1], i2cc[:, u:u + 1], swc2[:, 0:1], ALU.mult))
            OP("vector", lambda e: e.reciprocal(rcol[:, :], mcol[:, :]),
               deps=[DEP("gpsimd")])
            OP("gpsimd", lambda e: e.tensor_scalar(s2c[:, :], rcol[:, :], 127.0,
                                                   None, ALU.mult),
               deps=[DEP("vector")])
            vdep = DEP("gpsimd")
            fn1 = (lambda e, u=u: e.transpose(
                pb0[0:1, u * 128:(u + 1) * 128], s2c[:, :], ident32[:, :]))
            if first_tp:
                t_write(pb0, fn1, [vdep])
                first_tp = False
            else:
                OP("tensor", fn1, [vdep])
        consume(pb0, "vector", lambda e: e.tensor_copy(
            rowbuf[:, :], pb0[0:1, 0:MT]), deps=[DEP("tensor")])
        OP("vector", lambda e: e.drain())
        rdep = DEP("vector")
        t_write(pb0, lambda e: e.matmul(pb0[:, 0:MT], ones_r[:, :],
                                        rowbuf[:, 0:MT],
                                        start=True, stop=True), deps=[rdep])
        s2bc_t = DEP("tensor")
        qdeps = []
        for h in range(32):
            OP("vector", lambda e, h=h: e.tensor_tensor(
                khi[:, :], khtr[:, h, :], pb0[:, 0:MT], ALU.mult),
                deps=[s2bc_t])
            q = OP("vector", lambda e, h=h: e.tensor_copy(
                khtr[:, h, :], khi[:, :]))
            qdeps.append(DOP(q))
        bank_guard[id(pb0)] = [qdeps[-1]]
        return qdeps

    def w2_out_tile(m, qdeps):
        evs = []
        for k in range(8):
            p = pm0 if k % 2 == 0 else pm1
            for hh in range(32):
                lw = (kw2t[:, hh * D + k * 128:hh * D + (k + 1) * 128], False)
                fn = (lambda e, k=k, hh=hh, p=p: e.matmul(
                    p[:, 0:MT], kw2t[:, hh * D + k * 128:hh * D + (k + 1) * 128],
                    khtr[:, hh, :], start=(hh == 0), stop=(hh == 31)))
                if hh == 0:
                    i = t_write(p, fn, [qdeps[0]] if k == 0 else [], ldw=lw)
                elif k == 0:
                    i = OP("tensor", fn, [qdeps[hh]], ldw=lw)
                else:
                    i = t_acc(fn, ldw=lw)
            last_w2 = i
            ev = consume(p, "scalar", lambda e, k=k, p=p: e.activation(
                otb[:, k * MT:(k + 1) * MT], p[:, 0:MT], AF.Identity),
                deps=[DEP("tensor")])
            evs.append(DOP(ev))
        kreader = DOP(last_w2)
        for u in range(2):
            OP("sync", lambda e, m=m, u=u: e.dma_start(
                out=x2rt[:, :],
                in_=x2s[m * MT + u * 128:m * MT + (u + 1) * 128, :]),
                dma="DL", deps=[DEP("vector"), kx_ready])
            x2dep = DMA("DL")
            for ph in range(2):
                p = ptb0 if ph == 0 else ptb1
                for kk in range(4):
                    k = ph * 4 + kk
                    lw = (otb[:, k * MT + u * 128:k * MT + (u + 1) * 128], True)
                    fn = (lambda e, k=k, kk=kk, u=u, p=p: e.transpose(
                        p[:, kk * 128:(kk + 1) * 128],
                        otb[:, k * MT + u * 128:k * MT + (u + 1) * 128],
                        identb[:, :]))
                    if kk == 0:
                        t_write(p, fn, [evs[k]], ldw=lw)
                    else:
                        OP("tensor", fn, [evs[k]], ldw=lw)
            tdep = DEP("tensor")
            consume(ptb0, "vector", lambda e, u=u: e.scalar_tensor_tensor(
                outt[:, 0:512], ptb0[:, 0:512], i2cc[:, u:u + 1],
                gb_t[:, 0:512], ALU.mult, ALU.mult),
                deps=[tdep, sc.ddma("DO")])
            consume(ptb1, "vector", lambda e, u=u: e.scalar_tensor_tensor(
                outt[:, 512:1024], ptb1[:, 0:512], i2cc[:, u:u + 1],
                gb_t[:, 512:1024], ALU.mult, ALU.mult))
            OP("vector", lambda e: e.tensor_tensor(
                outt[:, :], outt[:, :], x2rt[:, :], ALU.add), deps=[x2dep])
            OP("scalar", lambda e, m=m, u=u: e.dma_start(
                out=out_d[m * MT + u * 128:m * MT + (u + 1) * 128, :],
                in_=outt[:, :]), dma="DO", deps=[DEP("vector")])
        return kreader

    evac_prev = [None] * 32
    for m in range(NMT + 1):
        evac_prev = merged_tile(m, evac_prev)
        if m >= 1:
            qd = mid_quant(m - 1)
            kht_reader[0] = w2_out_tile(m - 1, qd)
        if debug and m == 2:
            OP("sync", lambda e: e.dma_start(out=dbg_y, in_=y[1][:, :]), dma="DS",
               deps=[DEP("vector")])
    s2.close()

    OP("sync", None, deps=[sc.ddma("DO"), sc.ddma("DS"), sc.ddma("DL"),
                           sc.ddma("DG")])

    with nc.Block() as blk:
        sc.emit(blk)
    return nc


_NC_CACHE = {}


def _get_nc(debug=False):
    key = bool(debug)
    if key not in _NC_CACHE:
        _NC_CACHE[key] = build_nc(debug)
    return _NC_CACHE[key]


def make_in_maps(inputs):
    x = np.asarray(inputs["L0_attn_x"], np.float32)
    c = np.asarray(inputs["L0_c"], np.float32)
    identb = np.eye(128, dtype=ml_dtypes.bfloat16)
    ident32 = np.eye(128, dtype=np.float32)
    cwr = np.ascontiguousarray(np.asarray(inputs["L0_ffn_conv_w"],
                                          np.float32).reshape(DH, 3))
    common = {
        "wst": np.ascontiguousarray(inputs["L0_attn_W_state"], dtype=np.float32),
        "wgt": np.ascontiguousarray(inputs["L0_attn_W_gate"], dtype=np.float32),
        "bst_in": np.ascontiguousarray(inputs["L0_attn_b_state"], dtype=np.float32),
        "bgt_in": np.ascontiguousarray(inputs["L0_attn_b_gate"], dtype=np.float32),
        "w1": np.ascontiguousarray(inputs["L0_ffn_w1"], dtype=np.float32),
        "cw": cwr,
        "cbv": np.ascontiguousarray(inputs["L0_ffn_conv_b"], dtype=np.float32),
        "w2": np.ascontiguousarray(inputs["L0_ffn_w2"], dtype=np.float32),
        "wmod": np.ascontiguousarray(inputs["L0_wmod"], dtype=np.float32),
        "identb_in": identb,
        "ident32_in": ident32,
    }
    in_maps = []
    for core in range(8):
        b, q = core // 4, core % 4
        t0 = q * T - WM
        xqv = np.zeros((XR, D), np.float32)
        lo, hi = max(t0, 0), min(t0 + XR, 16384)
        xqv[lo - t0:hi - t0] = x[b, lo:hi]
        segmask = np.ones((128, 8 * SEG), np.float16)
        if q == 0:
            segmask[:, 0::SEG] = 0.0
        edgemask = np.ones((2, 1), np.float32)
        if q == 0:
            edgemask[0, 0] = 0.0
        if q == 3:
            edgemask[1, 0] = 0.0
        m = dict(common)
        m["xq"] = xqv
        m["cvec"] = np.ascontiguousarray(c[b])
        m["segmask_in"] = segmask
        m["edgemask_in"] = edgemask
        in_maps.append(m)
    return in_maps


def kernel(**inputs):
    nc = _get_nc(debug=False)
    in_maps = make_in_maps(inputs)
    res = run_bass_kernel_spmd(nc, in_maps, core_ids=list(range(8)))
    out = np.empty((2, 16384, D), np.float32)
    for core in range(8):
        b, q = core // 4, core % 4
        out[b, q * T:(q + 1) * T] = res.results[core]["out"]
    return out
